# version 23
# baseline (speedup 1.0000x reference)
import numpy as np
import ml_dtypes
import contextlib
import concourse.bass as bass
import concourse.mybir as mybir
from concourse.bass_utils import run_bass_kernel_spmd

F32 = mybir.dt.float32
BF16 = mybir.dt.bfloat16
AF = mybir.ActivationFunctionType
ALU = mybir.AluOpType
AX = mybir.AxisListType

MAGIC = 12582912.0
D = 1024
HID = 4096
DC = D // 128
CO = HID // 128
TOK = 4096
NCH = TOK // 128
TB = 256
NT = TOK // TB
TPS = NT // 2
XQS = 258
EPS = 1e-5
PREP = True
TRUNC = None
DBGX2 = False


class Prog:

    def __init__(self):
        self.ops = {"T": [], "V": [], "S": [], "G": [], "Y": []}
        self.cnt = {}
        self.waited = {}

    def n(self, s):
        return self.cnt.get(s, 0)

    def op(self, eng, fn, waits=(), inc=None, amt=1, raw_inc=False):
        resolved = []
        for s, v in waits:
            if v is None:
                v = self.cnt.get(s, 0)
            if v <= 0:
                continue
            k = (eng, s)
            if self.waited.get(k, 0) >= v:
                continue
            self.waited[k] = v
            resolved.append((s, v))
        if inc is not None:
            self.cnt[inc] = self.cnt.get(inc, 0) + amt
        self.seq = getattr(self, 'seq', 0) + 1
        self.ops[eng].append((self.seq, fn, resolved, None if raw_inc else inc, amt))


def build_program(detect_races=False):
    nc = bass.Bass("TRN2", target_bir_lowering=False,
                   detect_race_conditions=detect_races)

    xin = nc.dram_tensor("x", [TOK, D], F32, kind="ExternalInput").ap()
    wq_h = nc.dram_tensor("wq", [D, D], F32, kind="ExternalInput").ap()
    wk_h = nc.dram_tensor("wk", [D, D], F32, kind="ExternalInput").ap()
    wv_h = nc.dram_tensor("wv", [D, D], F32, kind="ExternalInput").ap()
    w1_h = nc.dram_tensor("w1", [HID, D], F32, kind="ExternalInput").ap()
    w2_h = nc.dram_tensor("w2", [D, HID], F32, kind="ExternalInput").ap()
    bqc_h = nc.dram_tensor("bqc", [128, DC], F32, kind="ExternalInput").ap()
    bkc_h = nc.dram_tensor("bkc", [128, DC], F32, kind="ExternalInput").ap()
    bvr_h = nc.dram_tensor("bvr", [1, D], F32, kind="ExternalInput").ap()
    cwc_h = nc.dram_tensor("cwc", [128, 3 * CO], F32, kind="ExternalInput").ap()
    cbc_h = nc.dram_tensor("cbc", [128, CO], F32, kind="ExternalInput").ap()
    idb_h = nc.dram_tensor("idb", [128, 128], BF16, kind="ExternalInput").ap()
    idf_h = nc.dram_tensor("idf", [128, 128], F32, kind="ExternalInput").ap()
    onr_h = nc.dram_tensor("onr", [1, 128], F32, kind="ExternalInput").ap()
    onc_h = nc.dram_tensor("onc", [128, 1], F32, kind="ExternalInput").ap()
    zec_h = nc.dram_tensor("zec", [128, 4], F32, kind="ExternalInput").ap()
    out_h = nc.dram_tensor("out", [TOK, D], F32, kind="ExternalOutput").ap()
    x2sc = nc.dram_tensor("x2sc", [TOK, D], F32, kind="Internal").ap()
    w1sc = nc.dram_tensor("w1sc", [128, CO * 1024], BF16, kind="Internal").ap()
    w2sc = nc.dram_tensor("w2sc", [128, DC * 4096], BF16, kind="Internal").ap()

    p = Prog()
    ctx = contextlib.ExitStack()
    E = ctx.enter_context
    sems = {}

    def sem(name):
        if name not in sems:
            sems[name] = E(nc.semaphore(name))
        return name

    def sb(name, shape, dt):
        return E(nc.sbuf_tensor(name, shape, dt))

    idb = sb("idb_s", [128, 128], BF16)
    idf = sb("idf_s", [128, 128], F32)
    onr = sb("onr_s", [1, 128], F32)
    onc = sb("onc_s", [128, 1], F32)
    zec = sb("zec_s", [128, 4], F32)
    cwc = sb("cwc_s", [128, 3 * CO], F32)
    cbc = sb("cbc_s", [128, CO], F32)
    m2col = sb("m2col", [128, 1], F32)
    r1col = sb("r1col", [128, 1], F32)
    r2col = sb("r2col", [128, 1], F32)
    m1col = sb("m1col", [128, 1], F32)
    acc1 = sb("acc1", [128, 1], F32)
    acc2 = sb("acc2", [128, 1], F32)
    tot11 = sb("tot11", [1, 1], F32)
    redt = sb("redt", [128, 1], F32)
    stt12_ = [sb(f"stt12_{j}", [128, 12], F32) for j in range(2)]
    sc4a_ = [sb(f"sc4a_{j}", [128, 4], F32) for j in range(2)]
    sc4b_ = [sb(f"sc4b_{j}", [128, 4], F32) for j in range(2)]
    sc4c_ = [sb(f"sc4c_{j}", [128, 4], F32) for j in range(2)]
    mu_c_ = [sb(f"mu_c_{j}", [128, 1], F32) for j in range(2)]
    pp_c_ = [sb(f"pp_c_{j}", [128, 1], F32) for j in range(2)]
    vv_c_ = [sb(f"vv_c_{j}", [128, 1], F32) for j in range(2)]
    rs_c_ = [sb(f"rs_c_{j}", [128, 1], F32) for j in range(2)]
    nrs_c_ = [sb(f"nrs_c_{j}", [128, 1], F32) for j in range(2)]
    nmr_c_ = [sb(f"nmr_c_{j}", [128, 1], F32) for j in range(2)]
    ln_idx = [0]
    amx_c = [sb(f"amx{j}", [128, 1], F32) for j in range(2)]
    s1_c = [sb(f"s1c{j}", [128, 1], F32) for j in range(2)]
    a1_c = [sb(f"a1c{j}", [128, 1], F32) for j in range(2)]
    am2_c = [sb(f"am2{j}", [128, 1], F32) for j in range(2)]
    s2_c = [sb(f"s2c{j}", [128, 1], F32) for j in range(2)]
    a2m2_c = [sb(f"a2m2{j}", [128, 1], F32) for j in range(2)]
    s2row = sb("s2row", [1, 128], F32)

    def ln_stats(src_ap, extra_v_waits):
        j = ln_idx[0] % 2
        ln_idx[0] += 1
        stt12, sc4a, sc4b, sc4c = stt12_[j], sc4a_[j], sc4b_[j], sc4c_[j]
        mu_c, pp_c, vv_c = mu_c_[j], pp_c_[j], vv_c_[j]
        rs_c, nrs_c, nmr_c = rs_c_[j], nrs_c_[j], nmr_c_[j]
        ls, lv, sf = f"_ls{j}", f"_lv{j}", f"_sf{j}"
        w = list(extra_v_waits) + [(sem(ls), None)]
        sem(lv), sem(sf)
        p.op("V", lambda v, a=src_ap, t=stt12: v.bn_stats(t[:, 0:6], a[:, 0:512]),
             w, inc=sem("_lf"))
        p.op("V", lambda v, a=src_ap, t=stt12: v.bn_stats(t[:, 6:12], a[:, 512:1024]),
             inc=sem("_lf"))
        means = stt12[:, 1:12:3]
        m2s = stt12[:, 2:12:3]
        p.op("V", lambda v, m=means, o=sc4a, mc=mu_c: v.tensor_scalar(
            o[:], m, 0.25, 0.0, ALU.mult, ALU.add, accum_out=mc[:]),
            [("_lf", None)], inc=sem("_lf"))
        p.op("V", lambda v, m=means, o=sc4b, pc=pp_c: v.scalar_tensor_tensor(
            o[:], m, 0.25, m, ALU.mult, ALU.mult, accum_out=pc[:]),
            inc=sem("_lf"))
        p.op("V", lambda v, m=m2s, o=sc4c, vc=vv_c: v.tensor_scalar(
            o[:], m, 1.0 / 1024.0, 0.0, ALU.mult, ALU.add, accum_out=vc[:]),
            inc=sem("_lf"))
        p.op("V", lambda v, pc=pp_c, vc=vv_c: v.tensor_add(pc[:], pc[:], vc[:]),
             [("_lf", None)], inc=sem("_lf"))
        p.op("V", lambda v, mc=mu_c, pc=pp_c, vc=vv_c: v.scalar_tensor_tensor(
            vc[:], mc[:], mc[:], pc[:], ALU.mult, ALU.subtract),
            [("_lf", None)])
        p.op("V", lambda v, nc_=nrs_c, mc=mu_c: v.tensor_sub(nc_[:], zec[:, 0:1],
                                                             mc[:]),
             inc=sem(lv))
        p.op("S", lambda s, vc=vv_c, rc=rs_c: s.activation(
            rc[:], vc[:], AF.Ln, bias=zec[:, 1:2], scale=zec[:, 3:4]),
            [(lv, None)],
            inc=sem(sf))
        p.op("S", lambda s, rc=rs_c: s.activation(
            rc[:], rc[:], AF.Exp, bias=0.0, scale=zec[:, 2:3]), [(sf, None)],
            inc=sem(sf))
        p.op("S", lambda s, nc_=nrs_c, mr=nmr_c, rc=rs_c: s.activation(
            mr[:], nc_[:], AF.Copy, bias=0.0, scale=rc[:]), [(sf, None)],
            inc=sem(sf))
        p.op("S", lambda s: s.nop(), [(sf, None)], inc=sem(ls))
        return rs_c, nmr_c

    actx = contextlib.ExitStack()
    EA = actx.enter_context

    def sba(name, shape, dt):
        return EA(nc.sbuf_tensor(name, shape, dt))

    def psa(name, shape):
        return EA(nc.psum_tensor(name, shape, F32))

    wqT = sba("wqT", [128, DC * 1024], BF16)
    wkT = sba("wkT", [128, DC * 1024], BF16)
    wvT = sba("wvT", [128, DC * 1024], BF16)
    bqc = sba("bqc_s", [128, DC], F32)
    bkc = sba("bkc_s", [128, DC], F32)
    bvb = sba("bvb", [128, D], F32)
    xbuf = [sba(f"xbuf{j}", [128, D], F32) for j in range(2)]
    x1n = [sba(f"x1n{j}", [128, D], BF16) for j in range(2)]
    x1nT = [sba(f"x1nT{j}", [128, D], BF16) for j in range(2)]
    qt_sb = sba("qt_sb", [128, D], BF16)
    kt_sb = sba("kt_sb", [128, D], BF16)
    v_sb = sba("v_sb", [128, D], BF16)
    p_sb = sba("p_sb", [128, 128], BF16)
    pt_sb = sba("pt_sb", [128, 128], BF16)
    o_sb = sba("o_sb", [128, D], F32)
    m_col = sba("m_col", [128, 1], F32)
    nm_col = sba("nm_col", [128, 1], F32)
    ss_col = sba("ss_col", [128, 1], F32)
    rr_col = sba("rr_col", [128, 1], F32)
    wstg = [sba(f"wstg{j}", [128, 1024], BF16) for j in range(2)]
    fstg = [sba(f"fstg{j}", [128, 1024], F32) for j in range(4)]
    tstg = [sba(f"tstg{j}", [128, 1024], BF16) for j in range(2)]
    ostg = [sba(f"ostg{j}", [128, 1024], BF16) for j in range(2)]

    qt_ps = psa("qt_ps", [128, 1024])
    kt_ps = psa("kt_ps", [128, 1024])
    v_ps = psa("v_ps", [128, 1024])
    fb = [psa(f"fb{j}", [128, 512]) for j in range(2)]
    fac = {"i": 0, "free": [[], []]}

    def fac_begin():
        b = fac["i"] % 2
        fac["i"] += 1
        return b, list(fac["free"][b])

    def fac_end(b, conds):
        fac["free"][b] = list(conds)

    for dst, src in ((idb, idb_h), (idf, idf_h), (onr, onr_h), (onc, onc_h),
                     (zec, zec_h), (cwc, cwc_h), (cbc, cbc_h), (bqc, bqc_h),
                     (bkc, bkc_h)):
        p.op("Y", lambda y, d=dst, s=src: y.dma_start(out=d[:], in_=s),
             inc=sem("cld"), amt=16)
    p.op("Y", lambda y: y.dma_start(out=bvb[0:1, :], in_=bvr_h),
         inc=sem("cld"), amt=16)
    CLD = p.n("cld")

    for h in range(2):
        p.op("T", lambda t, h=h: t.matmul(
            qt_ps[:, h * 512:(h + 1) * 512], onr[:],
            bvb[0:1, h * 512:(h + 1) * 512], start=True, stop=True),
            waits=[("cld", CLD)], inc=sem("bvt"))
    p.op("V", lambda v: v.tensor_copy(bvb[:], qt_ps[:]), waits=[("bvt", 2)],
         inc=sem("bvv"))

    for widx, (w_hbm, wT_sb) in enumerate(((wq_h, wqT), (wk_h, wkT), (wv_h, wvT)) if PREP else ()):
        for ec in range(DC):
            n = widx * 8 + ec
            slot = n % 2
            p.op("G", lambda g, s=slot, src=w_hbm, e=ec:
                 g.dma_start(out=wstg[s][:], in_=src[e * 128:(e + 1) * 128, :]),
                 waits=[("pT", max(0, (n - 1) * 8))],
                 inc=sem(f"gw{n % 2}"), amt=16)
            for d0 in range(0, DC, 4):
                b, fw = fac_begin()
                for j in range(4):
                    p.op("T", lambda t, s=slot, d=d0 + j, b=b, j=j:
                         t.matmul(fb[b][:, j * 128:(j + 1) * 128],
                                  wstg[s][:, d * 128:(d + 1) * 128],
                                  idb[:], start=True, stop=True),
                         waits=(fw if j == 0 else []) +
                               [(f"gw{n % 2}", 16 * (n // 2 + 1)), ("cld", CLD)],
                         inc=sem("pT"))
                p.op("S", lambda s_, dst=wT_sb, e=ec, d0=d0, b=b:
                     s_.copy(dst[:, e * 1024 + d0 * 128: e * 1024 + d0 * 128 + 512],
                             fb[b][:, 0:512]),
                     waits=[("pT", p.n("pT"))], inc=sem("pTe"))
                fac_end(b, [("pTe", p.n("pTe"))])
    QKV_PREP = p.n("pTe")

    fstg_free = [("cld", 0)] * 4
    nw_ld = [0]

    def mk_abs(src_ap, acc, first):
        pc = {}

        def load():
            n = nw_ld[0]
            nw_ld[0] += 1
            pc["n"] = n
            slot = n % 4
            p.op("Y", lambda y, s=slot, src=src_ap: y.dma_start(out=fstg[s][:],
                                                                in_=src),
                 waits=[fstg_free[slot]], inc=sem(f"yw{slot}"), amt=16)

        def mid():
            n = pc["n"]
            slot = n % 4
            p.op("V", lambda v, s=slot: v.reduce_sum(redt[:], fstg[s][:], AX.X,
                                                     apply_absolute_value=True),
                 waits=[(f"yw{slot}", 16 * (n // 4 + 1)),
                        ("absv", p.n("absv"))],
                 inc=sem("absr"))
            if first:
                p.op("G", lambda g, a=acc: g.tensor_copy(a[:], redt[:]),
                     waits=[("absr", p.n("absr"))], inc=sem("absv"))
            else:
                p.op("G", lambda g, a=acc: g.tensor_add(a[:], a[:], redt[:]),
                     waits=[("absr", p.n("absr"))], inc=sem("absv"))
            fstg_free[slot] = ("absv", p.n("absv"))

        def fin():
            pass

        return dict(load=load, mid=mid, fin=fin)

    def combine(acc, mcol, rcol, nabs):
        b, fw = fac_begin()
        p.op("T", lambda t, a=acc, b=b: t.matmul(
            fb[b][:1, 0:1], a[:], onc[:], start=True, stop=True),
            waits=fw + [("absv", nabs), ("cld", CLD)], inc=sem("pT"))
        p.op("V", lambda v, b=b: v.tensor_copy(tot11[:], fb[b][0:1, 0:1]),
             waits=[("pT", p.n("pT"))], inc=sem("_lf"))
        p.op("V", lambda v: v.tensor_scalar(tot11[:], tot11[:], 1.0 / (HID * D),
                                            1e-5, ALU.mult, ALU.max),
             waits=[("_lf", None)], inc=sem("cmbv"))
        fac_end(b, [("cmbv", p.n("cmbv"))])
        b2, fw2 = fac_begin()
        p.op("T", lambda t, b=b2: t.matmul(
            fb[b][:, 0:1], onr[:], tot11[:], start=True, stop=True),
            waits=fw2 + [("cmbv", p.n("cmbv"))], inc=sem("pT"))
        p.op("V", lambda v, d=mcol, b=b2: v.tensor_copy(d[:], fb[b][:, 0:1]),
             waits=[("pT", p.n("pT"))], inc=sem("_lf"))
        p.op("V", lambda v, d=rcol, mc=mcol: v.reciprocal(d[:], mc[:]),
             waits=[("_lf", None)], inc=sem("cmbv"))
        fac_end(b2, [("cmbv", p.n("cmbv"))])


    ntern = [0]

    def mk_tern(src_ap, rcol, scr, scol_off, mcl_need):
        pc = {}

        def load():
            n = nw_ld[0]
            nw_ld[0] += 1
            pc["n"] = n
            slot = n % 4
            p.op("Y", lambda y, s=slot, src=src_ap: y.dma_start(out=fstg[s][:],
                                                                in_=src),
                 waits=[fstg_free[slot]], inc=sem(f"yw{slot}"), amt=16)

        def mid():
            n = pc["n"]
            slot = n % 4
            nt = ntern[0]
            ntern[0] += 1
            pc["nt"] = nt
            ts_ = nt % 2
            p.op("V", lambda v, s=slot, r=rcol: v.tensor_scalar(
                fstg[s][:], fstg[s][:], r[:], MAGIC, ALU.mult, ALU.add),
                waits=[(f"yw{slot}", 16 * (n // 4 + 1)), ("cmbv", 2 * mcl_need)])
            p.op("V", lambda v, s=slot: v.tensor_scalar(
                fstg[s][:], fstg[s][:], MAGIC + 1.0, MAGIC - 1.0, ALU.min,
                ALU.max))
            p.op("V", lambda v, s=slot, t_=ts_: v.tensor_scalar(
                tstg[t_][:], fstg[s][:], MAGIC, None, ALU.subtract),
                waits=[("ternT", max(0, (nt - 1) * 8))], inc=sem("ternv"))
            fstg_free[slot] = ("ternv", p.n("ternv"))
            pc["tv"] = p.n("ternv")

        def fin():
            nt = pc["nt"]
            ts_ = nt % 2
            for d0 in range(0, DC, 4):
                b, fw = fac_begin()
                for j in range(4):
                    p.op("T", lambda t, t_=ts_, d=d0 + j, b=b, j=j:
                         t.matmul(fb[b][:, j * 128:(j + 1) * 128],
                                  tstg[t_][:, d * 128:(d + 1) * 128],
                                  idb[:], start=True, stop=True),
                         waits=(fw if j == 0 else []) + [("ternv", pc["tv"])],
                         inc=sem("ternT"))
                p.op("S", lambda s_, d0=d0, b=b, o=ostg[nt % 2]:
                     s_.copy(o[:, d0 * 128:d0 * 128 + 512], fb[b][:, 0:512]),
                     waits=[("ternT", p.n("ternT")),
                            (f"wst{nt % 2}", max(0, 16 * (nt // 2)))],
                     inc=sem("pTe"))
                fac_end(b, [("pTe", p.n("pTe"))])
            p.op("Y", lambda y, sc=scr, off=scol_off, o=ostg[nt % 2]:
                 y.dma_start(out=sc[:, off:off + 1024], in_=o[:]),
                 waits=[("pTe", p.n("pTe"))], inc=sem(f"wst{nt % 2}"), amt=16)

        return dict(load=load, mid=mid, fin=fin)

    w1_abs = [mk_abs(w1_h[co * 128:(co + 1) * 128, :], acc1, co == 0)
              for co in range(CO)]
    w2_abs = [mk_abs(w2_h[d * 128:(d + 1) * 128, c * 1024:(c + 1) * 1024], acc2,
                     d == 0 and c == 0) for d in range(DC) for c in range(4)]
    w1_tern = [mk_tern(w1_h[co * 128:(co + 1) * 128, :], r1col, w1sc,
                       co * 1024, 1) for co in range(CO)]
    w2_tern = [mk_tern(w2_h[d * 128:(d + 1) * 128, c * 1024:(c + 1) * 1024],
                       r2col, w2sc, d * 4096 + c * 1024, 2)
               for d in range(DC) for c in range(4)]

    def _cmb1fin():
        combine(acc1, m1col, r1col, 32)
        p.op("V", lambda v: v.tensor_scalar(cwc[:], cwc[:], m1col[:], None,
                                            ALU.mult),
             waits=[("cmbv", 2), ("cld", CLD)], inc=sem("cwr"))

    def _nostage():
        pass

    combine1 = dict(load=_nostage, mid=_nostage, fin=_cmb1fin)
    combine2 = dict(load=_nostage, mid=_nostage,
                    fin=lambda: combine(acc2, m2col, r2col, 64))

    prep_sched = {}
    for c in range(8 if PREP else 0):
        prep_sched.setdefault(c, []).extend(w1_abs[c * 4:(c + 1) * 4])
    if PREP:
        prep_sched.setdefault(8, []).append(combine1)
        for c in range(8, 16):
            prep_sched.setdefault(c, []).extend(w2_abs[(c - 8) * 4:(c - 7) * 4])
        prep_sched.setdefault(16, []).append(combine2)
        for i, un in enumerate(w1_tern):
            prep_sched.setdefault(9 + i // 3, []).append(un)
        for i, un in enumerate(w2_tern):
            prep_sched.setdefault(17 + i // 3, []).append(un)
    prep_pieces = []
    for c in sorted(prep_sched):
        prep_pieces.extend(prep_sched[c])
    pst = {"load": 0, "mid": 0, "fin": 0}

    def prep_advance():
        i_f = pst["fin"]
        np_ = len(prep_pieces)
        while pst["load"] <= min(i_f + 3, np_ - 1):
            prep_pieces[pst["load"]]["load"]()
            pst["load"] += 1
        while pst["mid"] <= min(i_f + 2, np_ - 1):
            prep_pieces[pst["mid"]]["mid"]()
            pst["mid"] += 1
        prep_pieces[i_f]["fin"]()
        pst["fin"] += 1

    def a_load(c):
        p.op("Y", lambda y, c=c: y.dma_start(
            out=xbuf[c % 2][:], in_=xin[c * 128:(c + 1) * 128, :]),
            waits=[(f"x2st{c % 2}", max(0, 16 * (c // 2)))],
            inc=sem(f"xld{c % 2}"), amt=16)

    def a_ln1(c):
        rs_c, nmr_c = ln_stats(xbuf[c % 2][:], [(f"xld{c % 2}", 16 * (c // 2 + 1))])
        p.op("S", lambda s, c=c, r=rs_c, m=nmr_c: s.activation(
            x1n[c % 2][:], xbuf[c % 2][:], AF.Identity, bias=m[:], scale=r[:]),
             waits=[("at1t", max(0, 8 * (c - 1)))], inc=sem("aln1"))

    def a_trans(c):
        for d0 in range(0, DC, 4):
            b, fw = fac_begin()
            for j in range(4):
                p.op("T", lambda t, c=c, dc=d0 + j, b=b, j=j:
                     t.matmul(fb[b][:, j * 128:(j + 1) * 128],
                              x1n[c % 2][:, dc * 128:(dc + 1) * 128], idb[:],
                              start=True, stop=True),
                     waits=(fw if j == 0 else []) + [("aln1", c + 1)],
                     inc=sem("at1t"))
            p.op("S", lambda s, c=c, d0=d0, b=b:
                 s.copy(x1nT[c % 2][:, d0 * 128:d0 * 128 + 512],
                        fb[b][:, 0:512]),
                 waits=[("at1t", p.n("at1t")), ("av_mm", max(0, c - 1))],
                 inc=sem("at1e"))
            fac_end(b, [("at1e", p.n("at1e"))])

    def a_qkv(c):
        xT = x1nT[c % 2]
        for ec in range(DC):
            for dc in range(DC):
                p.op("T", lambda t, e=ec, d=dc, x=xT:
                     t.matmul(qt_ps[:, e * 128:(e + 1) * 128],
                              wqT[:, e * 1024 + d * 128:e * 1024 + d * 128 + 128],
                              x[:, d * 128:(d + 1) * 128],
                              start=(d == 0), stop=(d == DC - 1)),
                     waits=[("at1e", 2 * (c + 1)), ("pTe", QKV_PREP),
                            ("aqe", max(0, c)), ("bvv", 1)],
                     inc=sem("aq") if (ec == DC - 1 and dc == DC - 1) else None)
        for ec in range(DC):
            for dc in range(DC):
                p.op("T", lambda t, e=ec, d=dc, x=xT:
                     t.matmul(kt_ps[:, e * 128:(e + 1) * 128],
                              wkT[:, e * 1024 + d * 128:e * 1024 + d * 128 + 128],
                              x[:, d * 128:(d + 1) * 128],
                              start=(d == 0), stop=(d == DC - 1)),
                     waits=[("ake", max(0, c))],
                     inc=sem("ak") if (ec == DC - 1 and dc == DC - 1) else None)
        wvr = wvT[:].rearrange("p (e x) -> p e x", x=1024)
        for dc in range(DC):
            for h in range(2):
                p.op("T", lambda t, d=dc, h=h, x=xT:
                     t.matmul(v_ps[:, h * 512:(h + 1) * 512],
                              x[:, d * 128:(d + 1) * 128],
                              wvr[:, h * 4:h * 4 + 4, d * 128:(d + 1) * 128],
                              start=(d == 0), stop=(d == DC - 1)),
                     waits=[("aoev", max(0, c))],
                     inc=sem("av_mm") if (dc == DC - 1 and h == 1) else None)

    def a_evac(c):
        for ec in range(DC):
            p.op("S", lambda s, e=ec: s.activation(
                qt_sb[:, e * 128:(e + 1) * 128], qt_ps[:, e * 128:(e + 1) * 128],
                AF.Identity, bias=bqc[:, e:e + 1], scale=1.0),
                waits=[("aq", c + 1), ("asc", max(0, c))],
                inc=sem("aqe") if ec == DC - 1 else None)
        for ec in range(DC):
            p.op("S", lambda s, e=ec: s.activation(
                kt_sb[:, e * 128:(e + 1) * 128], kt_ps[:, e * 128:(e + 1) * 128],
                AF.Identity, bias=bkc[:, e:e + 1], scale=1.0),
                waits=[("ak", c + 1)],
                inc=sem("ake") if ec == DC - 1 else None)
        p.op("V", lambda v: v.tensor_add(v_sb[:], v_ps[:], bvb[:]),
             waits=[("av_mm", c + 1), ("bvv", 1), ("aav", max(0, c))],
             inc=sem("ave"))

    def a_attn(c):
        bS, fwS = fac_begin()
        for ec in range(DC):
            p.op("T", lambda t, e=ec, b=bS: t.matmul(
                fb[b][:, 0:128], qt_sb[:, e * 128:(e + 1) * 128],
                kt_sb[:, e * 128:(e + 1) * 128],
                start=(e == 0), stop=(e == DC - 1)),
                waits=(fwS if ec == 0 else []) + [("aqe", c + 1), ("ake", c + 1)],
                inc=sem("asc") if ec == DC - 1 else None)
        p.op("V", lambda v, b=bS: v.reduce_max(nm_col[:], fb[b][:, 0:128], AX.X,
                                               negate=True),
             waits=[("asc", c + 1)], inc=sem("avmax"))
        p.op("S", lambda s, b=bS: s.activation(p_sb[:], fb[b][:, 0:128], AF.Exp,
                                               bias=nm_col[:], scale=1.0),
             waits=[("avmax", c + 1), ("apt", max(0, c))], inc=sem("aexp"))
        fac_end(bS, [("avmax", p.n("avmax")), ("aexp", p.n("aexp"))])
        bP, fwP = fac_begin()
        p.op("T", lambda t, b=bP: t.matmul(fb[b][:, 0:128], p_sb[:], idb[:],
                                           start=True, stop=True),
             waits=fwP + [("aexp", c + 1)], inc=sem("apt"))
        p.op("V", lambda v, b=bP: v.tensor_copy(pt_sb[:], fb[b][:, 0:128]),
             waits=[("apt", c + 1), ("aav", max(0, c))], inc=sem("apte"))
        fac_end(bP, [("apte", p.n("apte"))])
        for h in range(2):
            p.op("T", lambda t, h=h: t.matmul(
                v_ps[:, h * 512:(h + 1) * 512], pt_sb[:],
                v_sb[:, h * 512:(h + 1) * 512], start=True, stop=True),
                waits=[("apte", c + 1), ("ave", c + 1)],
                inc=sem("aav") if h == 1 else None)
        p.op("S", lambda s: s.copy(o_sb[:], v_ps[:]),
             waits=[("aav", c + 1), ("ax2v", max(0, c))],
             inc=sem("aoev"))

    def a_ln2(c):
        rs_c, nmr_c = ln_stats(o_sb[:], [("aoev", c + 1)])
        p.op("V", lambda v, c=c, m=nmr_c: v.tensor_scalar(
            xbuf[c % 2][:], xbuf[c % 2][:], m[:], None, ALU.add),
            waits=[(f"_ls{(ln_idx[0] - 1) % 2}", None)])
        p.op("V", lambda v, c=c, r=rs_c: v.scalar_tensor_tensor(
            xbuf[c % 2][:], o_sb[:], r[:], xbuf[c % 2][:], ALU.mult, ALU.add),
            inc=sem("ax2v"))
        p.op("Y", lambda y, c=c: y.dma_start(
            out=x2sc[c * 128:(c + 1) * 128, :], in_=xbuf[c % 2][:]),
            waits=[("ax2v", c + 1)], inc=sem(f"x2st{c % 2}"), amt=16)

    a_load(0)
    a_load(1)
    a_ln1(0)
    a_trans(0)
    for c in range(NCH):
        if c + 1 < NCH:
            a_ln1(c + 1)
        a_qkv(c)
        a_evac(c)
        a_attn(c)
        if c + 1 < NCH:
            a_trans(c + 1)
        a_ln2(c)
        if c + 2 < NCH:
            a_load(c + 2)
        for _ in prep_sched.get(c, []):
            prep_advance()

    abh = sem("ab")
    for e in ("T", "V", "S", "G", "Y"):
        p.op(e, lambda eng: eng.sem_inc(sems["ab"], 1),
             waits=[("x2st0", None), ("x2st1", None),
                    ("wst0", None), ("wst1", None)] if e == "Y" else [],
             inc="ab", raw_inc=True)

    actx.close()

    bctx = contextlib.ExitStack()
    EB = bctx.enter_context

    def sbb(name, shape, dt):
        return EB(nc.sbuf_tensor(name, shape, dt))

    def psb(name, shape):
        return EB(nc.psum_tensor(name, shape, F32))

    w1qT = sbb("w1qT", [128, CO * 1024], BF16)
    w2qT = sbb("w2qT", [128, DC * 4096], BF16)
    x2t = sbb("x2t", [128, 2 * D], F32)
    g_sb = sbb("g_sb", [128, CO * TB], F32)
    xqfm = [sbb(f"xqfm{j}", [128, DC * XQS], BF16) for j in range(3)]
    xqtm = [sbb(f"xqtm{j}", [128, D], BF16) for j in range(2)]
    fu = sbb("fu", [128, D], F32)
    scr1k = [sbb(f"scr1k{j}", [128, D], F32) for j in range(2)]
    c1buf = sbb("c1buf", [128, TB], F32)
    ybuf = [sbb(f"ybuf{j}", [128, TB], F32) for j in range(2)]
    xq2 = [sbb(f"xq2{j}", [128, 512], BF16) for j in range(2)]
    mred = scr1k[1][:, 256:512]
    s2b = sbb("s2b", [128, 512], F32)

    h_ps = [psb(f"h_ps{j}", [128, 512]) for j in range(2)]
    o_ps = [psb(f"o_ps{j}", [128, 1024]) for j in range(2)]
    f_ps = [psb(f"f_ps{j}", [128, 512]) for j in range(2)]
    hps4 = [h_ps[0], h_ps[1], f_ps[0], f_ps[1]]

    AB = [("ab", 5)]

    p.op("Y", lambda y: y.dma_start(out=w1qT[:], in_=w1sc),
         waits=AB, inc=sem("wb1"), amt=16)
    p.op("Y", lambda y: y.dma_start(out=w2qT[:], in_=w2sc), inc=sem("wb2"), amt=16)
    p.op("Y", lambda y: y.dma_start(out=out_h, in_=x2sc), inc=sem("bcp"), amt=16)

    def b_load(i):
        for tc in range(2):
            r0 = i * TB + tc * 128
            p.op("Y", lambda y, i=i, tc=tc, r0=r0:
                 y.dma_start(out=x2t[:, tc * D:(tc + 1) * D],
                             in_=x2sc[r0:r0 + 128, :]),
                 waits=AB + [("bfu", max(0, 2 * i))],
                 inc=sem(f"bx{i % 2}"), amt=16)

    def b_front(f):
        buf = xqfm[f % 3]
        xq_n = [0, 0]

        def do_block(b):
            blk = 2 * f + b
            src = x2t[:, b * D:(b + 1) * D]
            rs_c, nmr_c = ln_stats(src, AB + [(f"bx{f % 2}", 32 * (f // 2 + 1))])
            p.op("S", lambda s, src=src, b=b, r=rs_c, m=nmr_c: s.activation(
                fu[:], src, AF.Identity, bias=m[:], scale=r[:]),
                waits=[("bfxq", max(0, blk - 1))],
                inc=sem("bfu"))
            p.op("V", lambda v, b=b: v.reduce_max(amx_c[b][:], fu[:], AX.X,
                                                  apply_absolute_value=True),
                 waits=[("bfu", p.n("bfu")), ("bfb", max(0, blk - 1))],
                 inc=sem("_lf"))
            p.op("V", lambda v, b=b: v.tensor_scalar(amx_c[b][:], amx_c[b][:],
                                                     EPS, None, ALU.max),
                 waits=[("_lf", None)], inc=sem("_lf"))
            p.op("V", lambda v, b=b: v.reciprocal(s1_c[b][:], amx_c[b][:]),
                 waits=[("_lf", None)], inc=sem("_lf"))
            p.op("V", lambda v, b=b: v.tensor_scalar(s1_c[b][:], s1_c[b][:],
                                                     127.0, None, ALU.mult),
                 waits=[("_lf", None)], inc=sem("_lf"))
            p.op("V", lambda v, b=b: v.tensor_scalar(a1_c[b][:], amx_c[b][:],
                                                     1.0 / 127.0, None, ALU.mult),
                 waits=[("_lf", None)], inc=sem("bfam"))
            p.op("S", lambda s, b=b: s.activation(
                fu[:], fu[:], AF.Copy, bias=MAGIC, scale=s1_c[b][:]),
                waits=[("bfam", p.n("bfam"))], inc=sem("bfb"))
            p.op("V", lambda v, b=b: v.tensor_scalar(
                xqtm[b][:], fu[:], MAGIC, a1_c[b][:], ALU.subtract, ALU.mult),
                waits=[("bfb", p.n("bfb")), ("bftT", max(0, blk * 8 - 8))],
                inc=sem("bfxq"))
            xq_n[b] = p.n("bfxq")

        def emit_T():
            bv = buf[:].rearrange("p (d c) -> p d c", c=XQS)
            for b in range(2):
                blk = 2 * f + b
                for d0 in range(0, DC, 4):
                    n2 = blk * 2 + d0 // 4
                    bank = n2 % 2
                    for j in range(4):
                        p.op("T", lambda t, b=b, dc=d0 + j, bank=bank, j=j:
                             t.matmul(f_ps[bank][:, j * 128:(j + 1) * 128],
                                      xqtm[b][:, dc * 128:(dc + 1) * 128],
                                      idb[:], start=True, stop=True),
                             waits=AB + [("bfxq", xq_n[b]),
                                         ("bfte", max(0, n2 - 1)),
                                         ("bconvy", max(0, 32 * f - 32))],
                             inc=sem("bftT"))
                    p.op("S", lambda s, d0=d0, b=b, bank=bank, bv=bv:
                         s.copy(bv[:, d0:d0 + 4, 1 + b * 128: 1 + b * 128 + 128],
                                f_ps[bank][:, 0:512]),
                         waits=[("bftT", p.n("bftT")), ("bhl", max(0, f - 1)),
                                ("bhmm", max(0, 32 * (f - 2)))],
                         inc=sem("bfte"))
        return (lambda: do_block(0), lambda: do_block(1), emit_T)

    def b_halo_left(i):
        cur = xqfm[i % 3][:].rearrange("p (d c) -> p d c", c=XQS)
        if i % TPS == 0:
            p.op("V", lambda v, cur=cur: v.tensor_scalar(
                cur[:, :, 0:1][:, :, 0], idf[:, 0:DC], 0.0, None, ALU.mult),
                waits=[("bfte", 4 * (i + 1)), ("cld", CLD),
                       ("bhmm", max(0, 32 * (i - 2)))], inc=sem("bhl"))
        else:
            prev = xqfm[(i - 1) % 3][:].rearrange("p (d c) -> p d c", c=XQS)
            p.op("V", lambda v, cur=cur, prev=prev: v.tensor_copy(
                cur[:, :, 0:1][:, :, 0], prev[:, :, 256:257][:, :, 0]),
                waits=[("bfte", 4 * (i + 1)),
                       ("bhmm", max(0, 32 * (i - 2)))], inc=sem("bhl"))

    def b_halo_right(i):
        cur = xqfm[i % 3][:].rearrange("p (d c) -> p d c", c=XQS)
        if i % TPS == TPS - 1:
            p.op("V", lambda v, cur=cur: v.tensor_scalar(
                cur[:, :, 257:258][:, :, 0], idf[:, 0:DC], 0.0, None, ALU.mult),
                waits=[("bhl", i + 1), ("cld", CLD)], inc=sem("bhr"))
        else:
            nxt = xqfm[(i + 1) % 3][:].rearrange("p (d c) -> p d c", c=XQS)
            p.op("V", lambda v, cur=cur, nxt=nxt: v.tensor_copy(
                cur[:, :, 257:258][:, :, 0], nxt[:, :, 1:2][:, :, 0]),
                waits=[("bfte", 4 * (i + 2)), ("bhl", i + 1)], inc=sem("bhr"))

    def b_coloop(i, pend=None):
        buf = xqfm[i % 3]
        for co2 in range(CO // 2):
            if pend is not None:
                if co2 == 3:
                    pend[0]()
                elif co2 == 8:
                    pend[1]()
            pair = (2 * co2, 2 * co2 + 1)
            for co in pair:
                k = 32 * i + co
                hp = hps4[co % 4]
                for dc in range(DC):
                    w = [("bhl", i + 1), ("bhr", i + 1), ("wb1", 16),
                         ("bconvy", max(0, k - 3))]
                    if co == 0:
                        w.append(("bmsqv", max(0, 2 * i)))
                    if co == 1:
                        w += [("bs2r", max(0, 2 * i)), ("bs2e", max(0, 4 * i))]
                    if co in (2, 3):
                        w.append(("bfte", min(4 * (i + 2), 4 * NT)))
                    p.op("T", lambda t, co=co, dc=dc, hp=hp, buf=buf:
                         t.matmul(hp[:, 0:258],
                                  w1qT[:, co * 1024 + dc * 128:
                                       co * 1024 + dc * 128 + 128],
                                  buf[:, dc * XQS: dc * XQS + 258],
                                  start=(dc == 0), stop=(dc == DC - 1)),
                         waits=w,
                         inc=sem("bhmm") if dc == DC - 1 else None)
            for co in pair:
                k = 32 * i + co
                hp = hps4[co % 4]
                p.op("V", lambda v, co=co, hp=hp: v.tensor_scalar(
                    ybuf[co % 2][:], hp[:, 0:256], cwc[:, 3 * co:3 * co + 1],
                    None, ALU.mult),
                    waits=[("bhmm", k + 1), ("cwr", 1),
                           ("bsilu", max(0, k - 1))])
                p.op("V", lambda v, co=co, hp=hp: v.scalar_tensor_tensor(
                    c1buf[:], hp[:, 1:257], cwc[:, 3 * co + 1:3 * co + 2],
                    ybuf[co % 2][:], ALU.mult, ALU.add))
                p.op("V", lambda v, co=co, hp=hp: v.scalar_tensor_tensor(
                    ybuf[co % 2][:], hp[:, 2:258], cwc[:, 3 * co + 2:3 * co + 3],
                    c1buf[:], ALU.mult, ALU.add),
                    inc=sem("bconvy"))
            for co in pair:
                k = 32 * i + co
                p.op("S", lambda s, co=co, i=i: s.activation(
                    g_sb[:, co * TB:(co + 1) * TB], ybuf[co % 2][:], AF.Silu,
                    bias=cbc[:, co:co + 1], scale=1.0),
                    waits=[("bconvy", k + 1),
                           ("bxq2", max(0, 16 * (i - 1) + co // 2 + 1))],
                    inc=sem("bsilu"))
            if co % 8 == 7:
                q = co // 8
                gq = g_sb[:].rearrange("p (co t) -> p t co", t=TB)[:, :, q * 8:(q + 1) * 8]
                dst = mred if q == 0 else scr1k[1][:, 0:TB]
                wq = [("bsilu", 32 * i + 8 * (q + 1))]
                if q == 0:
                    wq.append(("bacc", max(0, 16 * (2 * (i - 1) + 2))))
                if q == 0:
                    wq.append(("bmsqT", max(0, 2 * i)))
                p.op("V", lambda v, gq=gq, dst=dst: v.reduce_max(
                    dst[:], gq, AX.X, apply_absolute_value=True), wq,
                    inc=sem("_lf"))
                if q > 0:
                    p.op("V", lambda v: v.tensor_max(mred[:], mred[:],
                                                     scr1k[1][:, 0:TB]),
                         [("_lf", None)],
                         inc=sem("bmrd") if q == 3 else None)
        if pend is not None:
            pend[2]()

    def b_quant2(i):
        for tc in range(2):
            dstA = h_ps[0][:, 384:512] if tc == 0 else h_ps[0][:, 256:384]
            p.op("T", lambda t, tc=tc, dstA=dstA: t.matmul(
                dstA, mred[:, tc * 128:(tc + 1) * 128], idf[:],
                start=True, stop=True),
                waits=[("bmrd", i + 1), ("bmsqv", max(0, 2 * i + tc)),
                       ("bhmm", 32 * (i + 1))],
                inc=sem("bmsqT"))
            p.op("V", lambda v, tc=tc, dstA=dstA: v.reduce_max(
                am2_c[tc][:], dstA, AX.X),
                waits=[("bmsqT", 2 * i + tc + 1),
                       ("bfev", max(0, 2 * (i - 1) + tc + 1)),
                       ("bs2T", max(0, 2 * (i - 1) + tc + 1))],
                inc=sem("_lf"))
            p.op("V", lambda v, tc=tc: v.tensor_scalar(am2_c[tc][:], am2_c[tc][:],
                                                       EPS, None, ALU.max),
                 waits=[("_lf", None)], inc=sem("_lf"))
            p.op("V", lambda v, tc=tc: v.reciprocal(s2_c[tc][:], am2_c[tc][:]),
                 waits=[("_lf", None)], inc=sem("_lf"))
            p.op("V", lambda v, tc=tc: v.tensor_scalar(s2_c[tc][:], s2_c[tc][:],
                                                       127.0, None, ALU.mult),
                 waits=[("_lf", None)], inc=sem("_lf"))
            p.op("V", lambda v, tc=tc: v.tensor_scalar(a2m2_c[tc][:], am2_c[tc][:],
                                                       1.0 / 127.0, None, ALU.mult),
                 waits=[("_lf", None)], inc=sem("_lf"))
            p.op("V", lambda v, tc=tc: v.tensor_mul(a2m2_c[tc][:], a2m2_c[tc][:],
                                                    m2col[:]),
                 waits=[("_lf", None)], inc=sem("bmsqv"))
            p.op("T", lambda t, tc=tc: t.matmul(
                h_ps[1][:1, 256:384], s2_c[tc][:], idf[:], start=True, stop=True),
                waits=[("bmsqv", p.n("bmsqv")), ("bs2r", max(0, 2 * i + tc - 1)),
                       ("bs2e", None), ("bhmm", 32 * (i + 1))],
                inc=sem("bs2T"))
            p.op("S", lambda s: s.copy(s2row[:], h_ps[1][0:1, 256:384]),
                 waits=[("bs2T", p.n("bs2T")), ("bs2bT", None)], inc=sem("bs2r"))
            p.op("T", lambda t: t.matmul(h_ps[1][:, 384:512], onr[:], s2row[:],
                                         start=True, stop=True),
                 waits=[("bs2r", p.n("bs2r")),
                        ("bs2e", max(0, 4 * i + 2 * tc))],
                 inc=sem("bs2bT"))
            for rep in range(2):
                p.op("S", lambda s, tc=tc, rep=rep: s.copy(
                    s2b[:, rep * 256 + tc * 128: rep * 256 + tc * 128 + 128],
                    h_ps[1][:, 384:512]),
                    waits=[("bs2bT", p.n("bs2bT")), ("bxq2", None)],
                    inc=sem("bs2e"))


    def b_ffn2(i):
        w2r = w2qT[:].rearrange("p (d x) -> p d x", x=4096)
        for grp in range(16):
            gk = 16 * i + grp
            p.op("V", lambda v, grp=grp: v.tensor_mul(
                scr1k[0][:, 0:512], g_sb[:, grp * 512:(grp + 1) * 512], s2b[:]),
                waits=([("bacc", max(0, 16 * (2 * (i - 1) + 1)))]
                       if grp == 0 else []) +
                      [("bs2e", 4 * (i + 1)),
                       ("bwmm", max(0, (32 * i + 2 * grp) - 2))])
            p.op("V", lambda v, grp=grp: v.tensor_scalar(
                xq2[grp % 2][:], scr1k[0][:, 0:512], MAGIC, MAGIC, ALU.add,
                ALU.subtract),
                inc=sem("bxq2"))
            for ci in range(2):
                co = grp * 2 + ci
                for tc in range(2):
                    for h in range(2):
                        w = [("bxq2", gk + 1), ("wb2", 16)]
                        if co == 0:
                            w.append(("bfev", max(0, 2 * i)))
                        p.op("T", lambda t, co=co, ci=ci, tc=tc, h=h, w2r=w2r,
                             grp=grp:
                             t.matmul(o_ps[tc][:, h * 512:(h + 1) * 512],
                                      xq2[grp % 2][:, ci * 256 + tc * 128:
                                                    ci * 256 + tc * 128 + 128],
                                      w2r[:, h * 4:h * 4 + 4,
                                          co * 128:(co + 1) * 128],
                                      start=(co == 0), stop=(co == CO - 1)),
                             waits=w,
                             inc=sem("bwmm") if (tc == 1 and h == 1) else None)

    def b_ln4(i):
        for tc in range(2):
            p.op("S", lambda s, tc=tc: s.activation(
                scr1k[tc][:], o_ps[tc][:], AF.Copy, bias=0.0,
                scale=a2m2_c[tc][:]),
                waits=[("bwmm", 32 * (i + 1)), ("bmsqv", 2 * (i + 1)),
                       ("bacc", max(0, 16 * (2 * (i - 1) + tc + 1)))],
                inc=sem("bfev"))
            rs_c, nmr_c = ln_stats(scr1k[tc][:], [("bfev", p.n("bfev"))])
            p.op("S", lambda s, tc=tc, r=rs_c, m=nmr_c: s.activation(
                scr1k[tc][:], scr1k[tc][:], AF.Identity, bias=m[:],
                scale=r[:]), inc=sem("bfn"))
            p.op("G", lambda g, tc=tc, i=i: g.dma_start(
                out=out_h[i * TB + tc * 128: i * TB + tc * 128 + 128, :],
                in_=scr1k[tc][:], accum_op=ALU.add),
                waits=[("bfn", p.n("bfn")), ("bcp", 16)],
                inc=sem("bacc"), amt=16)

    b_load(0)
    for _st in b_front(0):
        _st()
    b_load(1)
    for _st in b_front(1):
        _st()
    for i in range(NT):
        b_halo_left(i)
        b_halo_right(i)
        pend = None
        if i + 2 < NT:
            b_load(i + 2)
            pend = b_front(i + 2)
        b_coloop(i, pend)
        b_quant2(i)
        b_ffn2(i)
        b_ln4(i)

    if DBGX2:
        for c in range(NCH):
            p.op("Y", lambda y, c=c: y.dma_start(
                out=x2t[:, 0:D], in_=x2sc[c * 128:(c + 1) * 128, :]),
                waits=AB, inc=sem("dbg"), amt=16)
            p.op("Y", lambda y, c=c: y.dma_start(
                out=out_h[c * 128:(c + 1) * 128, :], in_=x2t[:, 0:D]),
                waits=[("dbg", 16 * (2 * c + 1))], inc=sem("dbg"), amt=16)
        p.op("Y", lambda y: y.nop(), waits=[("dbg", 32 * NCH)])
    p.op("Y", lambda y: y.nop(), waits=[("bacc", 16 * 2 * NT)])

    bctx.close()

    with nc.Block() as block:
        def mk(ename):
            def run(eng):
                for seq, fn, waits, inc, amt in p.ops[ename]:
                    if TRUNC is not None and seq > TRUNC:
                        continue
                    for s, v in waits:
                        eng.wait_ge(sems[s], v)
                    ins = fn(eng)
                    if inc is not None:
                        ins.then_inc(sems[inc], amt)
            return run
        block.sync(mk("Y"))
        block.gpsimd(mk("G"))
        block.tensor(mk("T"))
        block.vector(mk("V"))
        block.scalar(mk("S"))

    globals()["_SEMS"] = {k: v.num for k, v in sems.items()}
    ctx.close()
    return nc


_NC_CACHE = []


def kernel(L0_attn_x, L0_attn_Wq, L0_attn_bq, L0_attn_Wk, L0_attn_bk,
           L0_attn_Wv, L0_attn_bv, L0_ffn_w1, L0_ffn_conv_w, L0_ffn_conv_b,
           L0_ffn_w2):
    x = np.ascontiguousarray(np.asarray(L0_attn_x, dtype=np.float32))
    B, S, _ = x.shape
    assert (B, S) == (16, 2048)

    def colsplit(vec, n):
        return np.ascontiguousarray(np.asarray(vec, np.float32).reshape(n, 128).T)

    common = {
        "wq": np.ascontiguousarray(np.asarray(L0_attn_Wq, np.float32)),
        "wk": np.ascontiguousarray(np.asarray(L0_attn_Wk, np.float32)),
        "wv": np.ascontiguousarray(np.asarray(L0_attn_Wv, np.float32)),
        "w1": np.ascontiguousarray(np.asarray(L0_ffn_w1, np.float32)),
        "w2": np.ascontiguousarray(np.asarray(L0_ffn_w2, np.float32)),
        "bqc": colsplit(L0_attn_bq, DC),
        "bkc": colsplit(L0_attn_bk, DC),
        "bvr": np.ascontiguousarray(
            np.asarray(L0_attn_bv, np.float32).reshape(1, D)),
        "cwc": np.ascontiguousarray(
            np.asarray(L0_ffn_conv_w, np.float32).reshape(HID, 3)
            .reshape(CO, 128, 3).transpose(1, 0, 2).reshape(128, 3 * CO)),
        "cbc": np.ascontiguousarray(
            np.asarray(L0_ffn_conv_b, np.float32).reshape(CO, 128).T),
        "idb": np.eye(128, dtype=ml_dtypes.bfloat16),
        "idf": np.eye(128, dtype=np.float32),
        "onr": np.ones((1, 128), dtype=np.float32),
        "onc": np.ones((128, 1), dtype=np.float32),
        "zec": np.concatenate([np.zeros((128, 1), np.float32),
                               np.full((128, 1), EPS, np.float32),
                               np.full((128, 1), -0.5, np.float32),
                               np.full((128, 1), -1.0, np.float32)], axis=1),
    }

    if not _NC_CACHE:
        _NC_CACHE.append(build_program())
    nc = _NC_CACHE[0]
    in_maps = []
    for c in range(8):
        m = dict(common)
        m["x"] = np.ascontiguousarray(x[2 * c:2 * c + 2].reshape(TOK, D))
        in_maps.append(m)
    res = run_bass_kernel_spmd(nc, in_maps, core_ids=list(range(8)))
    out = np.concatenate([np.asarray(r["out"]).reshape(2, S, D)
                          for r in res.results], axis=0)
    return out


# version 24
# speedup vs baseline: 1.0332x; 1.0332x over previous
import numpy as np
import ml_dtypes
import contextlib
import concourse.bass as bass
import concourse.mybir as mybir
from concourse.bass_utils import run_bass_kernel_spmd

F32 = mybir.dt.float32
BF16 = mybir.dt.bfloat16
AF = mybir.ActivationFunctionType
ALU = mybir.AluOpType
AX = mybir.AxisListType

MAGIC = 12582912.0
D = 1024
HID = 4096
DC = D // 128
CO = HID // 128
TOK = 4096
NCH = TOK // 128
TB = 256
NT = TOK // TB
TPS = NT // 2
XQS = 258
EPS = 1e-5
PREP = True
TRUNC = None
DBGX2 = False


class Prog:

    def __init__(self):
        self.ops = {"T": [], "V": [], "S": [], "G": [], "Y": []}
        self.cnt = {}
        self.waited = {}

    def n(self, s):
        return self.cnt.get(s, 0)

    def op(self, eng, fn, waits=(), inc=None, amt=1, raw_inc=False):
        resolved = []
        for s, v in waits:
            if v is None:
                v = self.cnt.get(s, 0)
            if v <= 0:
                continue
            k = (eng, s)
            if self.waited.get(k, 0) >= v:
                continue
            self.waited[k] = v
            resolved.append((s, v))
        if inc is not None:
            self.cnt[inc] = self.cnt.get(inc, 0) + amt
        self.seq = getattr(self, 'seq', 0) + 1
        self.ops[eng].append((self.seq, fn, resolved, None if raw_inc else inc, amt))


def build_program(detect_races=False):
    nc = bass.Bass("TRN2", target_bir_lowering=False,
                   detect_race_conditions=detect_races)

    xin = nc.dram_tensor("x", [TOK, D], F32, kind="ExternalInput").ap()
    wq_h = nc.dram_tensor("wq", [D, D], F32, kind="ExternalInput").ap()
    wk_h = nc.dram_tensor("wk", [D, D], F32, kind="ExternalInput").ap()
    wv_h = nc.dram_tensor("wv", [D, D], F32, kind="ExternalInput").ap()
    w1_h = nc.dram_tensor("w1", [HID, D], F32, kind="ExternalInput").ap()
    w2_h = nc.dram_tensor("w2", [D, HID], F32, kind="ExternalInput").ap()
    bqc_h = nc.dram_tensor("bqc", [128, DC], F32, kind="ExternalInput").ap()
    bkc_h = nc.dram_tensor("bkc", [128, DC], F32, kind="ExternalInput").ap()
    bvr_h = nc.dram_tensor("bvr", [1, D], F32, kind="ExternalInput").ap()
    cwc_h = nc.dram_tensor("cwc", [128, 3 * CO], F32, kind="ExternalInput").ap()
    cbc_h = nc.dram_tensor("cbc", [128, CO], F32, kind="ExternalInput").ap()
    idb_h = nc.dram_tensor("idb", [128, 128], BF16, kind="ExternalInput").ap()
    idf_h = nc.dram_tensor("idf", [128, 128], F32, kind="ExternalInput").ap()
    onr_h = nc.dram_tensor("onr", [1, 128], F32, kind="ExternalInput").ap()
    onc_h = nc.dram_tensor("onc", [128, 1], F32, kind="ExternalInput").ap()
    zec_h = nc.dram_tensor("zec", [128, 4], F32, kind="ExternalInput").ap()
    out_h = nc.dram_tensor("out", [TOK, D], F32, kind="ExternalOutput").ap()
    x2sc = nc.dram_tensor("x2sc", [TOK, D], F32, kind="Internal").ap()
    w1sc = nc.dram_tensor("w1sc", [128, CO * 1024], BF16, kind="Internal").ap()
    w2sc = nc.dram_tensor("w2sc", [128, DC * 4096], BF16, kind="Internal").ap()

    p = Prog()
    ctx = contextlib.ExitStack()
    E = ctx.enter_context
    sems = {}

    def sem(name):
        if name not in sems:
            sems[name] = E(nc.semaphore(name))
        return name

    def sb(name, shape, dt):
        return E(nc.sbuf_tensor(name, shape, dt))

    idb = sb("idb_s", [128, 128], BF16)
    idf = sb("idf_s", [128, 128], F32)
    onr = sb("onr_s", [1, 128], F32)
    onc = sb("onc_s", [128, 1], F32)
    zec = sb("zec_s", [128, 4], F32)
    cwc = sb("cwc_s", [128, 3 * CO], F32)
    cbc = sb("cbc_s", [128, CO], F32)
    m2col = sb("m2col", [128, 1], F32)
    r1col = sb("r1col", [128, 1], F32)
    r2col = sb("r2col", [128, 1], F32)
    m1col = sb("m1col", [128, 1], F32)
    acc1 = sb("acc1", [128, 1], F32)
    acc2 = sb("acc2", [128, 1], F32)
    tot11 = sb("tot11", [1, 1], F32)
    redt = sb("redt", [128, 1], F32)
    stt12_ = [sb(f"stt12_{j}", [128, 12], F32) for j in range(2)]
    sc4a_ = [sb(f"sc4a_{j}", [128, 4], F32) for j in range(2)]
    sc4b_ = [sb(f"sc4b_{j}", [128, 4], F32) for j in range(2)]
    sc4c_ = [sb(f"sc4c_{j}", [128, 4], F32) for j in range(2)]
    mu_c_ = [sb(f"mu_c_{j}", [128, 1], F32) for j in range(2)]
    pp_c_ = [sb(f"pp_c_{j}", [128, 1], F32) for j in range(2)]
    vv_c_ = [sb(f"vv_c_{j}", [128, 1], F32) for j in range(2)]
    rs_c_ = [sb(f"rs_c_{j}", [128, 1], F32) for j in range(2)]
    nrs_c_ = [sb(f"nrs_c_{j}", [128, 1], F32) for j in range(2)]
    nmr_c_ = [sb(f"nmr_c_{j}", [128, 1], F32) for j in range(2)]
    ln_idx = [0]
    amx_c = [sb(f"amx{j}", [128, 1], F32) for j in range(2)]
    s1_c = [sb(f"s1c{j}", [128, 1], F32) for j in range(2)]
    a1_c = [sb(f"a1c{j}", [128, 1], F32) for j in range(2)]
    am2_c = [sb(f"am2{j}", [128, 1], F32) for j in range(2)]
    s2_c = [sb(f"s2c{j}", [128, 1], F32) for j in range(2)]
    a2m2_c = [sb(f"a2m2{j}", [128, 1], F32) for j in range(2)]
    s2row = sb("s2row", [1, 128], F32)

    def ln_stats(src_ap, extra_v_waits):
        j = ln_idx[0] % 2
        ln_idx[0] += 1
        stt12, sc4a, sc4b, sc4c = stt12_[j], sc4a_[j], sc4b_[j], sc4c_[j]
        mu_c, pp_c, vv_c = mu_c_[j], pp_c_[j], vv_c_[j]
        rs_c, nrs_c, nmr_c = rs_c_[j], nrs_c_[j], nmr_c_[j]
        ls, lv, sf = f"_ls{j}", f"_lv{j}", f"_sf{j}"
        w = list(extra_v_waits) + [(sem(ls), None)]
        sem(lv), sem(sf)
        p.op("V", lambda v, a=src_ap, t=stt12: v.bn_stats(t[:, 0:6], a[:, 0:512]),
             w, inc=sem("_lf"))
        p.op("V", lambda v, a=src_ap, t=stt12: v.bn_stats(t[:, 6:12], a[:, 512:1024]),
             inc=sem("_lf"))
        means = stt12[:, 1:12:3]
        m2s = stt12[:, 2:12:3]
        p.op("V", lambda v, m=means, o=sc4a, mc=mu_c: v.tensor_scalar(
            o[:], m, 0.25, 0.0, ALU.mult, ALU.add, accum_out=mc[:]),
            [("_lf", None)], inc=sem("_lf"))
        p.op("V", lambda v, m=means, o=sc4b, pc=pp_c: v.scalar_tensor_tensor(
            o[:], m, 0.25, m, ALU.mult, ALU.mult, accum_out=pc[:]),
            inc=sem("_lf"))
        p.op("V", lambda v, m=m2s, o=sc4c, vc=vv_c: v.tensor_scalar(
            o[:], m, 1.0 / 1024.0, 0.0, ALU.mult, ALU.add, accum_out=vc[:]),
            inc=sem("_lf"))
        p.op("V", lambda v, pc=pp_c, vc=vv_c: v.tensor_add(pc[:], pc[:], vc[:]),
             [("_lf", None)], inc=sem("_lf"))
        p.op("V", lambda v, mc=mu_c, pc=pp_c, vc=vv_c: v.scalar_tensor_tensor(
            vc[:], mc[:], mc[:], pc[:], ALU.mult, ALU.subtract),
            [("_lf", None)])
        p.op("V", lambda v, nc_=nrs_c, mc=mu_c: v.tensor_sub(nc_[:], zec[:, 0:1],
                                                             mc[:]),
             inc=sem(lv))
        p.op("S", lambda s, vc=vv_c, rc=rs_c: s.activation(
            rc[:], vc[:], AF.Ln, bias=zec[:, 1:2], scale=zec[:, 3:4]),
            [(lv, None)],
            inc=sem(sf))
        p.op("S", lambda s, rc=rs_c: s.activation(
            rc[:], rc[:], AF.Exp, bias=0.0, scale=zec[:, 2:3]), [(sf, None)],
            inc=sem(sf))
        p.op("S", lambda s, nc_=nrs_c, mr=nmr_c, rc=rs_c: s.activation(
            mr[:], nc_[:], AF.Copy, bias=0.0, scale=rc[:]), [(sf, None)],
            inc=sem(sf))
        p.op("S", lambda s: s.nop(), [(sf, None)], inc=sem(ls))
        return rs_c, nmr_c

    actx = contextlib.ExitStack()
    EA = actx.enter_context

    def sba(name, shape, dt):
        return EA(nc.sbuf_tensor(name, shape, dt))

    def psa(name, shape):
        return EA(nc.psum_tensor(name, shape, F32))

    wqT = sba("wqT", [128, DC * 1024], BF16)
    wkT = sba("wkT", [128, DC * 1024], BF16)
    wvT = sba("wvT", [128, DC * 1024], BF16)
    bqc = sba("bqc_s", [128, DC], F32)
    bkc = sba("bkc_s", [128, DC], F32)
    bvb = sba("bvb", [128, D], F32)
    xbuf = [sba(f"xbuf{j}", [128, D], F32) for j in range(2)]
    x1n = [sba(f"x1n{j}", [128, D], BF16) for j in range(2)]
    x1nT = [sba(f"x1nT{j}", [128, D], BF16) for j in range(2)]
    qt_sb = sba("qt_sb", [128, D], BF16)
    kt_sb = sba("kt_sb", [128, D], BF16)
    v_sb = sba("v_sb", [128, D], BF16)
    p_sb = sba("p_sb", [128, 128], BF16)
    pt_sb = sba("pt_sb", [128, 128], BF16)
    o_sb = sba("o_sb", [128, D], F32)
    m_col = sba("m_col", [128, 1], F32)
    nm_col = sba("nm_col", [128, 1], F32)
    ss_col = sba("ss_col", [128, 1], F32)
    rr_col = sba("rr_col", [128, 1], F32)
    wstg = [sba(f"wstg{j}", [128, 1024], BF16) for j in range(2)]
    fstg = [sba(f"fstg{j}", [128, 1024], F32) for j in range(4)]
    tstg = [sba(f"tstg{j}", [128, 1024], BF16) for j in range(3)]
    ostg = [sba(f"ostg{j}", [128, 1024], BF16) for j in range(2)]

    qt_ps = psa("qt_ps", [128, 1024])
    kt_ps = psa("kt_ps", [128, 1024])
    v_ps = psa("v_ps", [128, 1024])
    fb = [psa(f"fb{j}", [128, 512]) for j in range(2)]
    fac = {"i": 0, "free": [[], []]}

    def fac_begin():
        b = fac["i"] % 2
        fac["i"] += 1
        return b, list(fac["free"][b])

    def fac_end(b, conds):
        fac["free"][b] = list(conds)

    for dst, src in ((idb, idb_h), (idf, idf_h), (onr, onr_h), (onc, onc_h),
                     (zec, zec_h), (cwc, cwc_h), (cbc, cbc_h), (bqc, bqc_h),
                     (bkc, bkc_h)):
        p.op("Y", lambda y, d=dst, s=src: y.dma_start(out=d[:], in_=s),
             inc=sem("cld"), amt=16)
    p.op("Y", lambda y: y.dma_start(out=bvb[0:1, :], in_=bvr_h),
         inc=sem("cld"), amt=16)
    CLD = p.n("cld")

    for h in range(2):
        p.op("T", lambda t, h=h: t.matmul(
            qt_ps[:, h * 512:(h + 1) * 512], onr[:],
            bvb[0:1, h * 512:(h + 1) * 512], start=True, stop=True),
            waits=[("cld", CLD)], inc=sem("bvt"))
    p.op("V", lambda v: v.tensor_copy(bvb[:], qt_ps[:]), waits=[("bvt", 2)],
         inc=sem("bvv"))

    for widx, (w_hbm, wT_sb) in enumerate(((wq_h, wqT), (wk_h, wkT), (wv_h, wvT)) if PREP else ()):
        for ec in range(DC):
            n = widx * 8 + ec
            slot = n % 2
            p.op("G", lambda g, s=slot, src=w_hbm, e=ec:
                 g.dma_start(out=wstg[s][:], in_=src[e * 128:(e + 1) * 128, :]),
                 waits=[("pT", max(0, (n - 1) * 8))],
                 inc=sem(f"gw{n % 2}"), amt=16)
            for d0 in range(0, DC, 4):
                b, fw = fac_begin()
                for j in range(4):
                    p.op("T", lambda t, s=slot, d=d0 + j, b=b, j=j:
                         t.matmul(fb[b][:, j * 128:(j + 1) * 128],
                                  wstg[s][:, d * 128:(d + 1) * 128],
                                  idb[:], start=True, stop=True),
                         waits=(fw if j == 0 else []) +
                               [(f"gw{n % 2}", 16 * (n // 2 + 1)), ("cld", CLD)],
                         inc=sem("pT"))
                p.op("S", lambda s_, dst=wT_sb, e=ec, d0=d0, b=b:
                     s_.copy(dst[:, e * 1024 + d0 * 128: e * 1024 + d0 * 128 + 512],
                             fb[b][:, 0:512]),
                     waits=[("pT", p.n("pT"))], inc=sem("pTe"))
                fac_end(b, [("pTe", p.n("pTe"))])
    QKV_PREP = p.n("pTe")

    fstg_free = [("cld", 0)] * 4
    nw_ld = [0]

    def mk_abs(src_ap, acc, first):
        pc = {}

        def load():
            n = nw_ld[0]
            nw_ld[0] += 1
            pc["n"] = n
            slot = n % 4
            p.op("Y", lambda y, s=slot, src=src_ap: y.dma_start(out=fstg[s][:],
                                                                in_=src),
                 waits=[fstg_free[slot]], inc=sem(f"yw{slot}"), amt=16)

        def mid():
            n = pc["n"]
            slot = n % 4
            p.op("V", lambda v, s=slot: v.reduce_sum(redt[:], fstg[s][:], AX.X,
                                                     apply_absolute_value=True),
                 waits=[(f"yw{slot}", 16 * (n // 4 + 1)),
                        ("absv", p.n("absv"))],
                 inc=sem("absr"))
            if first:
                p.op("G", lambda g, a=acc: g.tensor_copy(a[:], redt[:]),
                     waits=[("absr", p.n("absr"))], inc=sem("absv"))
            else:
                p.op("G", lambda g, a=acc: g.tensor_add(a[:], a[:], redt[:]),
                     waits=[("absr", p.n("absr"))], inc=sem("absv"))
            fstg_free[slot] = ("absv", p.n("absv"))

        def fin():
            pass

        return dict(load=load, mid=mid, fin=fin)

    def combine(acc, mcol, rcol, nabs):
        b, fw = fac_begin()
        p.op("T", lambda t, a=acc, b=b: t.matmul(
            fb[b][:1, 0:1], a[:], onc[:], start=True, stop=True),
            waits=fw + [("absv", nabs), ("cld", CLD)], inc=sem("pT"))
        p.op("V", lambda v, b=b: v.tensor_copy(tot11[:], fb[b][0:1, 0:1]),
             waits=[("pT", p.n("pT"))], inc=sem("_lf"))
        p.op("V", lambda v: v.tensor_scalar(tot11[:], tot11[:], 1.0 / (HID * D),
                                            1e-5, ALU.mult, ALU.max),
             waits=[("_lf", None)], inc=sem("cmbv"))
        fac_end(b, [("cmbv", p.n("cmbv"))])
        b2, fw2 = fac_begin()
        p.op("T", lambda t, b=b2: t.matmul(
            fb[b][:, 0:1], onr[:], tot11[:], start=True, stop=True),
            waits=fw2 + [("cmbv", p.n("cmbv"))], inc=sem("pT"))
        p.op("V", lambda v, d=mcol, b=b2: v.tensor_copy(d[:], fb[b][:, 0:1]),
             waits=[("pT", p.n("pT"))], inc=sem("_lf"))
        p.op("V", lambda v, d=rcol, mc=mcol: v.reciprocal(d[:], mc[:]),
             waits=[("_lf", None)], inc=sem("cmbv"))
        fac_end(b2, [("cmbv", p.n("cmbv"))])


    ntern = [0]

    def mk_tern(src_ap, rcol, scr, scol_off, mcl_need):
        pc = {}

        def load():
            n = nw_ld[0]
            nw_ld[0] += 1
            pc["n"] = n
            slot = n % 4
            p.op("Y", lambda y, s=slot, src=src_ap: y.dma_start(out=fstg[s][:],
                                                                in_=src),
                 waits=[fstg_free[slot]], inc=sem(f"yw{slot}"), amt=16)

        def mid():
            n = pc["n"]
            slot = n % 4
            nt = ntern[0]
            ntern[0] += 1
            pc["nt"] = nt
            ts_ = nt % 3
            p.op("V", lambda v, s=slot, r=rcol: v.tensor_scalar(
                fstg[s][:], fstg[s][:], r[:], MAGIC, ALU.mult, ALU.add),
                waits=[(f"yw{slot}", 16 * (n // 4 + 1)), ("cmbv", 2 * mcl_need)])
            p.op("V", lambda v, s=slot: v.tensor_scalar(
                fstg[s][:], fstg[s][:], MAGIC + 1.0, MAGIC - 1.0, ALU.min,
                ALU.max))
            p.op("V", lambda v, s=slot, t_=ts_: v.tensor_scalar(
                tstg[t_][:], fstg[s][:], MAGIC, None, ALU.subtract),
                waits=[("ternT", max(0, (nt - 2) * 8))], inc=sem("ternv"))
            fstg_free[slot] = ("ternv", p.n("ternv"))
            pc["tv"] = p.n("ternv")

        def fin():
            nt = pc["nt"]
            ts_ = nt % 3
            for d0 in range(0, DC, 4):
                b, fw = fac_begin()
                for j in range(4):
                    p.op("T", lambda t, t_=ts_, d=d0 + j, b=b, j=j:
                         t.matmul(fb[b][:, j * 128:(j + 1) * 128],
                                  tstg[t_][:, d * 128:(d + 1) * 128],
                                  idb[:], start=True, stop=True),
                         waits=(fw if j == 0 else []) + [("ternv", pc["tv"])],
                         inc=sem("ternT"))
                p.op("S", lambda s_, d0=d0, b=b, o=ostg[nt % 2]:
                     s_.copy(o[:, d0 * 128:d0 * 128 + 512], fb[b][:, 0:512]),
                     waits=[("ternT", p.n("ternT")),
                            (f"wst{nt % 2}", max(0, 16 * (nt // 2)))],
                     inc=sem("pTe"))
                fac_end(b, [("pTe", p.n("pTe"))])
            p.op("Y", lambda y, sc=scr, off=scol_off, o=ostg[nt % 2]:
                 y.dma_start(out=sc[:, off:off + 1024], in_=o[:]),
                 waits=[("pTe", p.n("pTe"))], inc=sem(f"wst{nt % 2}"), amt=16)

        return dict(load=load, mid=mid, fin=fin)

    w1_abs = [mk_abs(w1_h[co * 128:(co + 1) * 128, :], acc1, co == 0)
              for co in range(CO)]
    w2_abs = [mk_abs(w2_h[d * 128:(d + 1) * 128, c * 1024:(c + 1) * 1024], acc2,
                     d == 0 and c == 0) for d in range(DC) for c in range(4)]
    w1_tern = [mk_tern(w1_h[co * 128:(co + 1) * 128, :], r1col, w1sc,
                       co * 1024, 1) for co in range(CO)]
    w2_tern = [mk_tern(w2_h[d * 128:(d + 1) * 128, c * 1024:(c + 1) * 1024],
                       r2col, w2sc, d * 4096 + c * 1024, 2)
               for d in range(DC) for c in range(4)]

    def _cmb1fin():
        combine(acc1, m1col, r1col, 32)
        p.op("V", lambda v: v.tensor_scalar(cwc[:], cwc[:], m1col[:], None,
                                            ALU.mult),
             waits=[("cmbv", 2), ("cld", CLD)], inc=sem("cwr"))

    def _nostage():
        pass

    combine1 = dict(load=_nostage, mid=_nostage, fin=_cmb1fin)
    combine2 = dict(load=_nostage, mid=_nostage,
                    fin=lambda: combine(acc2, m2col, r2col, 64))

    prep_sched = {}
    for c in range(8 if PREP else 0):
        prep_sched.setdefault(c, []).extend(w1_abs[c * 4:(c + 1) * 4])
    if PREP:
        prep_sched.setdefault(8, []).append(combine1)
        for c in range(8, 16):
            prep_sched.setdefault(c, []).extend(w2_abs[(c - 8) * 4:(c - 7) * 4])
        prep_sched.setdefault(16, []).append(combine2)
        for i, un in enumerate(w1_tern):
            prep_sched.setdefault(9 + i // 3, []).append(un)
        for i, un in enumerate(w2_tern):
            prep_sched.setdefault(17 + i // 3, []).append(un)
    prep_pieces = []
    for c in sorted(prep_sched):
        prep_pieces.extend(prep_sched[c])
    pst = {"load": 0, "mid": 0, "fin": 0}

    def prep_advance():
        i_f = pst["fin"]
        np_ = len(prep_pieces)
        while pst["load"] <= min(i_f + 4, np_ - 1):
            prep_pieces[pst["load"]]["load"]()
            pst["load"] += 1
        while pst["mid"] <= min(i_f + 3, np_ - 1):
            prep_pieces[pst["mid"]]["mid"]()
            pst["mid"] += 1
        prep_pieces[i_f]["fin"]()
        pst["fin"] += 1

    def a_load(c):
        p.op("Y", lambda y, c=c: y.dma_start(
            out=xbuf[c % 2][:], in_=xin[c * 128:(c + 1) * 128, :]),
            waits=[(f"x2st{c % 2}", max(0, 16 * (c // 2)))],
            inc=sem(f"xld{c % 2}"), amt=16)

    def a_ln1(c):
        rs_c, nmr_c = ln_stats(xbuf[c % 2][:], [(f"xld{c % 2}", 16 * (c // 2 + 1))])
        p.op("S", lambda s, c=c, r=rs_c, m=nmr_c: s.activation(
            x1n[c % 2][:], xbuf[c % 2][:], AF.Identity, bias=m[:], scale=r[:]),
             waits=[("at1t", max(0, 8 * (c - 1)))], inc=sem("aln1"))

    def a_trans(c):
        for d0 in range(0, DC, 4):
            b, fw = fac_begin()
            for j in range(4):
                p.op("T", lambda t, c=c, dc=d0 + j, b=b, j=j:
                     t.matmul(fb[b][:, j * 128:(j + 1) * 128],
                              x1n[c % 2][:, dc * 128:(dc + 1) * 128], idb[:],
                              start=True, stop=True),
                     waits=(fw if j == 0 else []) + [("aln1", c + 1)],
                     inc=sem("at1t"))
            p.op("S", lambda s, c=c, d0=d0, b=b:
                 s.copy(x1nT[c % 2][:, d0 * 128:d0 * 128 + 512],
                        fb[b][:, 0:512]),
                 waits=[("at1t", p.n("at1t")), ("av_mm", max(0, c - 1))],
                 inc=sem("at1e"))
            fac_end(b, [("at1e", p.n("at1e"))])

    def a_qkv(c):
        xT = x1nT[c % 2]
        for ec in range(DC):
            for dc in range(DC):
                p.op("T", lambda t, e=ec, d=dc, x=xT:
                     t.matmul(qt_ps[:, e * 128:(e + 1) * 128],
                              wqT[:, e * 1024 + d * 128:e * 1024 + d * 128 + 128],
                              x[:, d * 128:(d + 1) * 128],
                              start=(d == 0), stop=(d == DC - 1)),
                     waits=[("at1e", 2 * (c + 1)), ("pTe", QKV_PREP),
                            ("aqe", max(0, c)), ("bvv", 1)],
                     inc=sem("aq") if (ec == DC - 1 and dc == DC - 1) else None)
        for ec in range(DC):
            for dc in range(DC):
                p.op("T", lambda t, e=ec, d=dc, x=xT:
                     t.matmul(kt_ps[:, e * 128:(e + 1) * 128],
                              wkT[:, e * 1024 + d * 128:e * 1024 + d * 128 + 128],
                              x[:, d * 128:(d + 1) * 128],
                              start=(d == 0), stop=(d == DC - 1)),
                     waits=[("ake", max(0, c))],
                     inc=sem("ak") if (ec == DC - 1 and dc == DC - 1) else None)
        wvr = wvT[:].rearrange("p (e x) -> p e x", x=1024)
        for dc in range(DC):
            for h in range(2):
                p.op("T", lambda t, d=dc, h=h, x=xT:
                     t.matmul(v_ps[:, h * 512:(h + 1) * 512],
                              x[:, d * 128:(d + 1) * 128],
                              wvr[:, h * 4:h * 4 + 4, d * 128:(d + 1) * 128],
                              start=(d == 0), stop=(d == DC - 1)),
                     waits=[("aoev", max(0, c))],
                     inc=sem("av_mm") if (dc == DC - 1 and h == 1) else None)

    def a_evac(c):
        for ec in range(DC):
            p.op("S", lambda s, e=ec: s.activation(
                qt_sb[:, e * 128:(e + 1) * 128], qt_ps[:, e * 128:(e + 1) * 128],
                AF.Identity, bias=bqc[:, e:e + 1], scale=1.0),
                waits=[("aq", c + 1), ("asc", max(0, c))],
                inc=sem("aqe") if ec == DC - 1 else None)
        for ec in range(DC):
            p.op("S", lambda s, e=ec: s.activation(
                kt_sb[:, e * 128:(e + 1) * 128], kt_ps[:, e * 128:(e + 1) * 128],
                AF.Identity, bias=bkc[:, e:e + 1], scale=1.0),
                waits=[("ak", c + 1)],
                inc=sem("ake") if ec == DC - 1 else None)
        p.op("V", lambda v: v.tensor_add(v_sb[:], v_ps[:], bvb[:]),
             waits=[("av_mm", c + 1), ("bvv", 1), ("aav", max(0, c))],
             inc=sem("ave"))

    def a_attn(c):
        bS, fwS = fac_begin()
        for ec in range(DC):
            p.op("T", lambda t, e=ec, b=bS: t.matmul(
                fb[b][:, 0:128], qt_sb[:, e * 128:(e + 1) * 128],
                kt_sb[:, e * 128:(e + 1) * 128],
                start=(e == 0), stop=(e == DC - 1)),
                waits=(fwS if ec == 0 else []) + [("aqe", c + 1), ("ake", c + 1)],
                inc=sem("asc") if ec == DC - 1 else None)
        p.op("V", lambda v, b=bS: v.reduce_max(nm_col[:], fb[b][:, 0:128], AX.X,
                                               negate=True),
             waits=[("asc", c + 1)], inc=sem("avmax"))
        p.op("S", lambda s, b=bS: s.activation(p_sb[:], fb[b][:, 0:128], AF.Exp,
                                               bias=nm_col[:], scale=1.0),
             waits=[("avmax", c + 1), ("apt", max(0, c))], inc=sem("aexp"))
        fac_end(bS, [("avmax", p.n("avmax")), ("aexp", p.n("aexp"))])
        bP, fwP = fac_begin()
        p.op("T", lambda t, b=bP: t.matmul(fb[b][:, 0:128], p_sb[:], idb[:],
                                           start=True, stop=True),
             waits=fwP + [("aexp", c + 1)], inc=sem("apt"))
        p.op("V", lambda v, b=bP: v.tensor_copy(pt_sb[:], fb[b][:, 0:128]),
             waits=[("apt", c + 1), ("aav", max(0, c))], inc=sem("apte"))
        fac_end(bP, [("apte", p.n("apte"))])
        for h in range(2):
            p.op("T", lambda t, h=h: t.matmul(
                v_ps[:, h * 512:(h + 1) * 512], pt_sb[:],
                v_sb[:, h * 512:(h + 1) * 512], start=True, stop=True),
                waits=[("apte", c + 1), ("ave", c + 1)],
                inc=sem("aav") if h == 1 else None)
        p.op("S", lambda s: s.copy(o_sb[:], v_ps[:]),
             waits=[("aav", c + 1), ("ax2v", max(0, c))],
             inc=sem("aoev"))

    def a_ln2(c):
        rs_c, nmr_c = ln_stats(o_sb[:], [("aoev", c + 1)])
        p.op("V", lambda v, c=c, m=nmr_c: v.tensor_scalar(
            xbuf[c % 2][:], xbuf[c % 2][:], m[:], None, ALU.add),
            waits=[(f"_ls{(ln_idx[0] - 1) % 2}", None)])
        p.op("V", lambda v, c=c, r=rs_c: v.scalar_tensor_tensor(
            xbuf[c % 2][:], o_sb[:], r[:], xbuf[c % 2][:], ALU.mult, ALU.add),
            inc=sem("ax2v"))
        p.op("Y", lambda y, c=c: y.dma_start(
            out=x2sc[c * 128:(c + 1) * 128, :], in_=xbuf[c % 2][:]),
            waits=[("ax2v", c + 1)], inc=sem(f"x2st{c % 2}"), amt=16)

    a_load(0)
    a_load(1)
    a_ln1(0)
    a_trans(0)
    for c in range(NCH):
        if c + 1 < NCH:
            a_ln1(c + 1)
        a_qkv(c)
        a_evac(c)
        a_attn(c)
        if c + 1 < NCH:
            a_trans(c + 1)
        a_ln2(c)
        if c + 2 < NCH:
            a_load(c + 2)
        for _ in prep_sched.get(c, []):
            prep_advance()

    abh = sem("ab")
    for e in ("T", "V", "S", "G", "Y"):
        p.op(e, lambda eng: eng.sem_inc(sems["ab"], 1),
             waits=[("x2st0", None), ("x2st1", None),
                    ("wst0", None), ("wst1", None)] if e == "Y" else [],
             inc="ab", raw_inc=True)

    actx.close()

    bctx = contextlib.ExitStack()
    EB = bctx.enter_context

    def sbb(name, shape, dt):
        return EB(nc.sbuf_tensor(name, shape, dt))

    def psb(name, shape):
        return EB(nc.psum_tensor(name, shape, F32))

    w1qT = sbb("w1qT", [128, CO * 1024], BF16)
    w2qT = sbb("w2qT", [128, DC * 4096], BF16)
    x2t = sbb("x2t", [128, 2 * D], F32)
    g_sb = sbb("g_sb", [128, CO * TB], F32)
    xqfm = [sbb(f"xqfm{j}", [128, DC * XQS], BF16) for j in range(3)]
    xqtm = [sbb(f"xqtm{j}", [128, D], BF16) for j in range(2)]
    fu = sbb("fu", [128, D], F32)
    scr1k = [sbb(f"scr1k{j}", [128, D], F32) for j in range(2)]
    c1buf = sbb("c1buf", [128, TB], F32)
    ybuf = [sbb(f"ybuf{j}", [128, TB], F32) for j in range(2)]
    xq2 = [sbb(f"xq2{j}", [128, 512], BF16) for j in range(2)]
    mred = scr1k[1][:, 256:512]
    s2b = sbb("s2b", [128, 512], F32)

    h_ps = [psb(f"h_ps{j}", [128, 512]) for j in range(2)]
    o_ps = [psb(f"o_ps{j}", [128, 1024]) for j in range(2)]
    f_ps = [psb(f"f_ps{j}", [128, 512]) for j in range(2)]
    hps4 = [h_ps[0], h_ps[1], f_ps[0], f_ps[1]]

    AB = [("ab", 5)]

    p.op("Y", lambda y: y.dma_start(out=w1qT[:], in_=w1sc),
         waits=AB, inc=sem("wb1"), amt=16)
    p.op("Y", lambda y: y.dma_start(out=w2qT[:], in_=w2sc), inc=sem("wb2"), amt=16)
    p.op("Y", lambda y: y.dma_start(out=out_h, in_=x2sc), inc=sem("bcp"), amt=16)

    def b_load(i):
        for tc in range(2):
            r0 = i * TB + tc * 128
            p.op("Y", lambda y, i=i, tc=tc, r0=r0:
                 y.dma_start(out=x2t[:, tc * D:(tc + 1) * D],
                             in_=x2sc[r0:r0 + 128, :]),
                 waits=AB + [("bfu", max(0, 2 * i))],
                 inc=sem(f"bx{i % 2}"), amt=16)

    def b_front(f):
        buf = xqfm[f % 3]
        xq_n = [0, 0]
        for b in range(2):
            blk = 2 * f + b
            src = x2t[:, b * D:(b + 1) * D]
            rs_c, nmr_c = ln_stats(src, AB + [(f"bx{f % 2}", 32 * (f // 2 + 1))])
            p.op("S", lambda s, src=src, b=b, r=rs_c, m=nmr_c: s.activation(
                fu[:], src, AF.Identity, bias=m[:], scale=r[:]),
                waits=[("bfxq", max(0, blk - 1))],
                inc=sem("bfu"))
            p.op("V", lambda v, b=b: v.reduce_max(amx_c[b][:], fu[:], AX.X,
                                                  apply_absolute_value=True),
                 waits=[("bfu", p.n("bfu")), ("bfb", max(0, blk - 1))],
                 inc=sem("_lf"))
            p.op("V", lambda v, b=b: v.tensor_scalar(amx_c[b][:], amx_c[b][:],
                                                     EPS, None, ALU.max),
                 waits=[("_lf", None)], inc=sem("_lf"))
            p.op("V", lambda v, b=b: v.reciprocal(s1_c[b][:], amx_c[b][:]),
                 waits=[("_lf", None)], inc=sem("_lf"))
            p.op("V", lambda v, b=b: v.tensor_scalar(s1_c[b][:], s1_c[b][:],
                                                     127.0, None, ALU.mult),
                 waits=[("_lf", None)], inc=sem("_lf"))
            p.op("V", lambda v, b=b: v.tensor_scalar(a1_c[b][:], amx_c[b][:],
                                                     1.0 / 127.0, None, ALU.mult),
                 waits=[("_lf", None)], inc=sem("bfam"))
            p.op("S", lambda s, b=b: s.activation(
                fu[:], fu[:], AF.Copy, bias=MAGIC, scale=s1_c[b][:]),
                waits=[("bfam", p.n("bfam"))], inc=sem("bfb"))
            p.op("V", lambda v, b=b: v.tensor_scalar(
                xqtm[b][:], fu[:], MAGIC, a1_c[b][:], ALU.subtract, ALU.mult),
                waits=[("bfb", p.n("bfb")), ("bftT", max(0, blk * 8 - 8))],
                inc=sem("bfxq"))
            xq_n[b] = p.n("bfxq")

        def emit_T():
            bv = buf[:].rearrange("p (d c) -> p d c", c=XQS)
            for b in range(2):
                blk = 2 * f + b
                for d0 in range(0, DC, 4):
                    n2 = blk * 2 + d0 // 4
                    bank = n2 % 2
                    for j in range(4):
                        p.op("T", lambda t, b=b, dc=d0 + j, bank=bank, j=j:
                             t.matmul(f_ps[bank][:, j * 128:(j + 1) * 128],
                                      xqtm[b][:, dc * 128:(dc + 1) * 128],
                                      idb[:], start=True, stop=True),
                             waits=AB + [("bfxq", xq_n[b]),
                                         ("bfte", max(0, n2 - 1)),
                                         ("bconvy", max(0, 32 * f - 32))],
                             inc=sem("bftT"))
                    p.op("S", lambda s, d0=d0, b=b, bank=bank, bv=bv:
                         s.copy(bv[:, d0:d0 + 4, 1 + b * 128: 1 + b * 128 + 128],
                                f_ps[bank][:, 0:512]),
                         waits=[("bftT", p.n("bftT")), ("bhl", max(0, f - 1)),
                                ("bhmm", max(0, 32 * (f - 2)))],
                         inc=sem("bfte"))
        return emit_T

    def b_halo_left(i):
        cur = xqfm[i % 3][:].rearrange("p (d c) -> p d c", c=XQS)
        if i % TPS == 0:
            p.op("V", lambda v, cur=cur: v.tensor_scalar(
                cur[:, :, 0:1][:, :, 0], idf[:, 0:DC], 0.0, None, ALU.mult),
                waits=[("bfte", 4 * (i + 1)), ("cld", CLD),
                       ("bhmm", max(0, 32 * (i - 2)))], inc=sem("bhl"))
        else:
            prev = xqfm[(i - 1) % 3][:].rearrange("p (d c) -> p d c", c=XQS)
            p.op("V", lambda v, cur=cur, prev=prev: v.tensor_copy(
                cur[:, :, 0:1][:, :, 0], prev[:, :, 256:257][:, :, 0]),
                waits=[("bfte", 4 * (i + 1)),
                       ("bhmm", max(0, 32 * (i - 2)))], inc=sem("bhl"))

    def b_halo_right(i):
        cur = xqfm[i % 3][:].rearrange("p (d c) -> p d c", c=XQS)
        if i % TPS == TPS - 1:
            p.op("V", lambda v, cur=cur: v.tensor_scalar(
                cur[:, :, 257:258][:, :, 0], idf[:, 0:DC], 0.0, None, ALU.mult),
                waits=[("bhl", i + 1), ("cld", CLD)], inc=sem("bhr"))
        else:
            nxt = xqfm[(i + 1) % 3][:].rearrange("p (d c) -> p d c", c=XQS)
            p.op("V", lambda v, cur=cur, nxt=nxt: v.tensor_copy(
                cur[:, :, 257:258][:, :, 0], nxt[:, :, 1:2][:, :, 0]),
                waits=[("bfte", 4 * (i + 2)), ("bhl", i + 1)], inc=sem("bhr"))

    def b_coloop(i, pend=None):
        buf = xqfm[i % 3]
        for co2 in range(CO // 2):
            pair = (2 * co2, 2 * co2 + 1)
            for co in pair:
                k = 32 * i + co
                hp = hps4[co % 4]
                for dc in range(DC):
                    w = [("bhl", i + 1), ("bhr", i + 1), ("wb1", 16),
                         ("bconvy", max(0, k - 3))]
                    if co == 0:
                        w.append(("bmsqv", max(0, 2 * i)))
                    if co == 1:
                        w += [("bs2r", max(0, 2 * i)), ("bs2e", max(0, 4 * i))]
                    if co in (2, 3):
                        w.append(("bfte", min(4 * (i + 2), 4 * NT)))
                    p.op("T", lambda t, co=co, dc=dc, hp=hp, buf=buf:
                         t.matmul(hp[:, 0:258],
                                  w1qT[:, co * 1024 + dc * 128:
                                       co * 1024 + dc * 128 + 128],
                                  buf[:, dc * XQS: dc * XQS + 258],
                                  start=(dc == 0), stop=(dc == DC - 1)),
                         waits=w,
                         inc=sem("bhmm") if dc == DC - 1 else None)
            for co in pair:
                k = 32 * i + co
                hp = hps4[co % 4]
                p.op("V", lambda v, co=co, hp=hp: v.tensor_scalar(
                    ybuf[co % 2][:], hp[:, 0:256], cwc[:, 3 * co:3 * co + 1],
                    None, ALU.mult),
                    waits=[("bhmm", k + 1), ("cwr", 1),
                           ("bsilu", max(0, k - 1))])
                p.op("V", lambda v, co=co, hp=hp: v.scalar_tensor_tensor(
                    c1buf[:], hp[:, 1:257], cwc[:, 3 * co + 1:3 * co + 2],
                    ybuf[co % 2][:], ALU.mult, ALU.add))
                p.op("V", lambda v, co=co, hp=hp: v.scalar_tensor_tensor(
                    ybuf[co % 2][:], hp[:, 2:258], cwc[:, 3 * co + 2:3 * co + 3],
                    c1buf[:], ALU.mult, ALU.add),
                    inc=sem("bconvy"))
            for co in pair:
                k = 32 * i + co
                p.op("S", lambda s, co=co, i=i: s.activation(
                    g_sb[:, co * TB:(co + 1) * TB], ybuf[co % 2][:], AF.Silu,
                    bias=cbc[:, co:co + 1], scale=1.0),
                    waits=[("bconvy", k + 1),
                           ("bxq2", max(0, 16 * (i - 1) + co // 2 + 1))],
                    inc=sem("bsilu"))
            if co % 8 == 7:
                q = co // 8
                gq = g_sb[:].rearrange("p (co t) -> p t co", t=TB)[:, :, q * 8:(q + 1) * 8]
                dst = mred if q == 0 else scr1k[1][:, 0:TB]
                wq = [("bsilu", 32 * i + 8 * (q + 1))]
                if q == 0:
                    wq.append(("bacc", max(0, 16 * (2 * (i - 1) + 2))))
                if q == 0:
                    wq.append(("bmsqT", max(0, 2 * i)))
                p.op("V", lambda v, gq=gq, dst=dst: v.reduce_max(
                    dst[:], gq, AX.X, apply_absolute_value=True), wq,
                    inc=sem("_lf"))
                if q > 0:
                    p.op("V", lambda v: v.tensor_max(mred[:], mred[:],
                                                     scr1k[1][:, 0:TB]),
                         [("_lf", None)],
                         inc=sem("bmrd") if q == 3 else None)
        if pend is not None:
            pend()

    def b_quant2(i):
        for tc in range(2):
            dstA = h_ps[0][:, 384:512] if tc == 0 else h_ps[0][:, 256:384]
            p.op("T", lambda t, tc=tc, dstA=dstA: t.matmul(
                dstA, mred[:, tc * 128:(tc + 1) * 128], idf[:],
                start=True, stop=True),
                waits=[("bmrd", i + 1), ("bmsqv", max(0, 2 * i + tc)),
                       ("bhmm", 32 * (i + 1))],
                inc=sem("bmsqT"))
            p.op("V", lambda v, tc=tc, dstA=dstA: v.reduce_max(
                am2_c[tc][:], dstA, AX.X),
                waits=[("bmsqT", 2 * i + tc + 1),
                       ("bfev", max(0, 2 * (i - 1) + tc + 1)),
                       ("bs2T", max(0, 2 * (i - 1) + tc + 1))],
                inc=sem("_lf"))
            p.op("V", lambda v, tc=tc: v.tensor_scalar(am2_c[tc][:], am2_c[tc][:],
                                                       EPS, None, ALU.max),
                 waits=[("_lf", None)], inc=sem("_lf"))
            p.op("V", lambda v, tc=tc: v.reciprocal(s2_c[tc][:], am2_c[tc][:]),
                 waits=[("_lf", None)], inc=sem("_lf"))
            p.op("V", lambda v, tc=tc: v.tensor_scalar(s2_c[tc][:], s2_c[tc][:],
                                                       127.0, None, ALU.mult),
                 waits=[("_lf", None)], inc=sem("_lf"))
            p.op("V", lambda v, tc=tc: v.tensor_scalar(a2m2_c[tc][:], am2_c[tc][:],
                                                       1.0 / 127.0, None, ALU.mult),
                 waits=[("_lf", None)], inc=sem("_lf"))
            p.op("V", lambda v, tc=tc: v.tensor_mul(a2m2_c[tc][:], a2m2_c[tc][:],
                                                    m2col[:]),
                 waits=[("_lf", None)], inc=sem("bmsqv"))
            p.op("T", lambda t, tc=tc: t.matmul(
                h_ps[1][:1, 256:384], s2_c[tc][:], idf[:], start=True, stop=True),
                waits=[("bmsqv", p.n("bmsqv")), ("bs2r", max(0, 2 * i + tc - 1)),
                       ("bs2e", None), ("bhmm", 32 * (i + 1))],
                inc=sem("bs2T"))
            p.op("V", lambda v: v.tensor_copy(s2row[:], h_ps[1][0:1, 256:384]),
                 waits=[("bs2T", p.n("bs2T")), ("bs2bT", None)], inc=sem("bs2r"))
            p.op("T", lambda t: t.matmul(h_ps[1][:, 384:512], onr[:], s2row[:],
                                         start=True, stop=True),
                 waits=[("bs2r", p.n("bs2r")),
                        ("bs2e", max(0, 4 * i + 2 * tc))],
                 inc=sem("bs2bT"))
            for rep in range(2):
                p.op("V", lambda v, tc=tc, rep=rep: v.tensor_copy(
                    s2b[:, rep * 256 + tc * 128: rep * 256 + tc * 128 + 128],
                    h_ps[1][:, 384:512]),
                    waits=[("bs2bT", p.n("bs2bT")), ("bxq2", None)],
                    inc=sem("bs2e"))


    def b_ffn2(i):
        w2r = w2qT[:].rearrange("p (d x) -> p d x", x=4096)
        for grp in range(16):
            gk = 16 * i + grp
            p.op("V", lambda v, grp=grp: v.tensor_mul(
                scr1k[0][:, 0:512], g_sb[:, grp * 512:(grp + 1) * 512], s2b[:]),
                waits=([("bacc", max(0, 16 * (2 * (i - 1) + 1)))]
                       if grp == 0 else []) +
                      [("bs2e", 4 * (i + 1)),
                       ("bwmm", max(0, (32 * i + 2 * grp) - 2))])
            p.op("V", lambda v, grp=grp: v.tensor_scalar(
                xq2[grp % 2][:], scr1k[0][:, 0:512], MAGIC, MAGIC, ALU.add,
                ALU.subtract),
                inc=sem("bxq2"))
            for ci in range(2):
                co = grp * 2 + ci
                for tc in range(2):
                    for h in range(2):
                        w = [("bxq2", gk + 1), ("wb2", 16)]
                        if co == 0:
                            w.append(("bfev", max(0, 2 * i)))
                        p.op("T", lambda t, co=co, ci=ci, tc=tc, h=h, w2r=w2r,
                             grp=grp:
                             t.matmul(o_ps[tc][:, h * 512:(h + 1) * 512],
                                      xq2[grp % 2][:, ci * 256 + tc * 128:
                                                    ci * 256 + tc * 128 + 128],
                                      w2r[:, h * 4:h * 4 + 4,
                                          co * 128:(co + 1) * 128],
                                      start=(co == 0), stop=(co == CO - 1)),
                             waits=w,
                             inc=sem("bwmm") if (tc == 1 and h == 1) else None)

    def b_ln4(i):
        for tc in range(2):
            p.op("S", lambda s, tc=tc: s.activation(
                scr1k[tc][:], o_ps[tc][:], AF.Copy, bias=0.0,
                scale=a2m2_c[tc][:]),
                waits=[("bwmm", 32 * (i + 1)), ("bmsqv", 2 * (i + 1)),
                       ("bacc", max(0, 16 * (2 * (i - 1) + tc + 1)))],
                inc=sem("bfev"))
            rs_c, nmr_c = ln_stats(scr1k[tc][:], [("bfev", p.n("bfev"))])
            p.op("S", lambda s, tc=tc, r=rs_c, m=nmr_c: s.activation(
                scr1k[tc][:], scr1k[tc][:], AF.Identity, bias=m[:],
                scale=r[:]), inc=sem("bfn"))
            p.op("G", lambda g, tc=tc, i=i: g.dma_start(
                out=out_h[i * TB + tc * 128: i * TB + tc * 128 + 128, :],
                in_=scr1k[tc][:], accum_op=ALU.add),
                waits=[("bfn", p.n("bfn")), ("bcp", 16)],
                inc=sem("bacc"), amt=16)

    b_load(0)
    b_front(0)()
    b_load(1)
    b_front(1)()
    for i in range(NT):
        b_halo_left(i)
        b_halo_right(i)
        pend = None
        if i + 2 < NT:
            b_load(i + 2)
            pend = b_front(i + 2)
        b_coloop(i, pend)
        b_quant2(i)
        b_ffn2(i)
        b_ln4(i)

    if DBGX2:
        for c in range(NCH):
            p.op("Y", lambda y, c=c: y.dma_start(
                out=x2t[:, 0:D], in_=x2sc[c * 128:(c + 1) * 128, :]),
                waits=AB, inc=sem("dbg"), amt=16)
            p.op("Y", lambda y, c=c: y.dma_start(
                out=out_h[c * 128:(c + 1) * 128, :], in_=x2t[:, 0:D]),
                waits=[("dbg", 16 * (2 * c + 1))], inc=sem("dbg"), amt=16)
        p.op("Y", lambda y: y.nop(), waits=[("dbg", 32 * NCH)])
    p.op("Y", lambda y: y.nop(), waits=[("bacc", 16 * 2 * NT)])

    bctx.close()

    with nc.Block() as block:
        def mk(ename):
            def run(eng):
                for seq, fn, waits, inc, amt in p.ops[ename]:
                    if TRUNC is not None and seq > TRUNC:
                        continue
                    for s, v in waits:
                        eng.wait_ge(sems[s], v)
                    ins = fn(eng)
                    if inc is not None:
                        ins.then_inc(sems[inc], amt)
            return run
        block.sync(mk("Y"))
        block.gpsimd(mk("G"))
        block.tensor(mk("T"))
        block.vector(mk("V"))
        block.scalar(mk("S"))

    globals()["_SEMS"] = {k: v.num for k, v in sems.items()}
    ctx.close()
    return nc


_NC_CACHE = []


def kernel(L0_attn_x, L0_attn_Wq, L0_attn_bq, L0_attn_Wk, L0_attn_bk,
           L0_attn_Wv, L0_attn_bv, L0_ffn_w1, L0_ffn_conv_w, L0_ffn_conv_b,
           L0_ffn_w2):
    x = np.ascontiguousarray(np.asarray(L0_attn_x, dtype=np.float32))
    B, S, _ = x.shape
    assert (B, S) == (16, 2048)

    def colsplit(vec, n):
        return np.ascontiguousarray(np.asarray(vec, np.float32).reshape(n, 128).T)

    common = {
        "wq": np.ascontiguousarray(np.asarray(L0_attn_Wq, np.float32)),
        "wk": np.ascontiguousarray(np.asarray(L0_attn_Wk, np.float32)),
        "wv": np.ascontiguousarray(np.asarray(L0_attn_Wv, np.float32)),
        "w1": np.ascontiguousarray(np.asarray(L0_ffn_w1, np.float32)),
        "w2": np.ascontiguousarray(np.asarray(L0_ffn_w2, np.float32)),
        "bqc": colsplit(L0_attn_bq, DC),
        "bkc": colsplit(L0_attn_bk, DC),
        "bvr": np.ascontiguousarray(
            np.asarray(L0_attn_bv, np.float32).reshape(1, D)),
        "cwc": np.ascontiguousarray(
            np.asarray(L0_ffn_conv_w, np.float32).reshape(HID, 3)
            .reshape(CO, 128, 3).transpose(1, 0, 2).reshape(128, 3 * CO)),
        "cbc": np.ascontiguousarray(
            np.asarray(L0_ffn_conv_b, np.float32).reshape(CO, 128).T),
        "idb": np.eye(128, dtype=ml_dtypes.bfloat16),
        "idf": np.eye(128, dtype=np.float32),
        "onr": np.ones((1, 128), dtype=np.float32),
        "onc": np.ones((128, 1), dtype=np.float32),
        "zec": np.concatenate([np.zeros((128, 1), np.float32),
                               np.full((128, 1), EPS, np.float32),
                               np.full((128, 1), -0.5, np.float32),
                               np.full((128, 1), -1.0, np.float32)], axis=1),
    }

    if not _NC_CACHE:
        _NC_CACHE.append(build_program())
    nc = _NC_CACHE[0]
    in_maps = []
    for c in range(8):
        m = dict(common)
        m["x"] = np.ascontiguousarray(x[2 * c:2 * c + 2].reshape(TOK, D))
        in_maps.append(m)
    res = run_bass_kernel_spmd(nc, in_maps, core_ids=list(range(8)))
    out = np.concatenate([np.asarray(r["out"]).reshape(2, S, D)
                          for r in res.results], axis=0)
    return out


# version 25 (final)
# speedup vs baseline: 1.0630x; 1.0288x over previous
import numpy as np
import ml_dtypes
import contextlib
import concourse.bass as bass
import concourse.mybir as mybir
from concourse.bass_utils import run_bass_kernel_spmd

F32 = mybir.dt.float32
BF16 = mybir.dt.bfloat16
AF = mybir.ActivationFunctionType
ALU = mybir.AluOpType
AX = mybir.AxisListType

MAGIC = 12582912.0
D = 1024
HID = 4096
DC = D // 128
CO = HID // 128
TOK = 4096
NCH = TOK // 128
TB = 256
NT = TOK // TB
TPS = NT // 2
XQS = 258
EPS = 1e-5
PREP = True
TRUNC = None
DBGX2 = False


class Prog:

    def __init__(self):
        self.ops = {"T": [], "V": [], "S": [], "G": [], "Y": []}
        self.cnt = {}
        self.waited = {}

    def n(self, s):
        return self.cnt.get(s, 0)

    def op(self, eng, fn, waits=(), inc=None, amt=1, raw_inc=False):
        resolved = []
        for s, v in waits:
            if v is None:
                v = self.cnt.get(s, 0)
            if v <= 0:
                continue
            k = (eng, s)
            if self.waited.get(k, 0) >= v:
                continue
            self.waited[k] = v
            resolved.append((s, v))
        if inc is not None:
            self.cnt[inc] = self.cnt.get(inc, 0) + amt
        self.seq = getattr(self, 'seq', 0) + 1
        self.ops[eng].append((self.seq, fn, resolved, None if raw_inc else inc, amt))


def build_program(detect_races=False):
    nc = bass.Bass("TRN2", target_bir_lowering=False,
                   detect_race_conditions=detect_races)

    xin = nc.dram_tensor("x", [TOK, D], F32, kind="ExternalInput").ap()
    wq_h = nc.dram_tensor("wq", [D, D], F32, kind="ExternalInput").ap()
    wk_h = nc.dram_tensor("wk", [D, D], F32, kind="ExternalInput").ap()
    wv_h = nc.dram_tensor("wv", [D, D], F32, kind="ExternalInput").ap()
    w1_h = nc.dram_tensor("w1", [HID, D], F32, kind="ExternalInput").ap()
    w2_h = nc.dram_tensor("w2", [D, HID], F32, kind="ExternalInput").ap()
    bqc_h = nc.dram_tensor("bqc", [128, DC], F32, kind="ExternalInput").ap()
    bkc_h = nc.dram_tensor("bkc", [128, DC], F32, kind="ExternalInput").ap()
    bvr_h = nc.dram_tensor("bvr", [1, D], F32, kind="ExternalInput").ap()
    cwc_h = nc.dram_tensor("cwc", [128, 3 * CO], F32, kind="ExternalInput").ap()
    cbc_h = nc.dram_tensor("cbc", [128, CO], F32, kind="ExternalInput").ap()
    idb_h = nc.dram_tensor("idb", [128, 128], BF16, kind="ExternalInput").ap()
    idf_h = nc.dram_tensor("idf", [128, 128], F32, kind="ExternalInput").ap()
    onr_h = nc.dram_tensor("onr", [1, 128], F32, kind="ExternalInput").ap()
    onc_h = nc.dram_tensor("onc", [128, 1], F32, kind="ExternalInput").ap()
    zec_h = nc.dram_tensor("zec", [128, 4], F32, kind="ExternalInput").ap()
    out_h = nc.dram_tensor("out", [TOK, D], F32, kind="ExternalOutput").ap()
    x2sc = nc.dram_tensor("x2sc", [TOK, D], F32, kind="Internal").ap()
    w1sc = nc.dram_tensor("w1sc", [128, CO * 1024], BF16, kind="Internal").ap()
    w2sc = nc.dram_tensor("w2sc", [128, DC * 4096], BF16, kind="Internal").ap()

    p = Prog()
    ctx = contextlib.ExitStack()
    E = ctx.enter_context
    sems = {}

    def sem(name):
        if name not in sems:
            sems[name] = E(nc.semaphore(name))
        return name

    def sb(name, shape, dt):
        return E(nc.sbuf_tensor(name, shape, dt))

    idb = sb("idb_s", [128, 128], BF16)
    idf = sb("idf_s", [128, 128], F32)
    onr = sb("onr_s", [1, 128], F32)
    onc = sb("onc_s", [128, 1], F32)
    zec = sb("zec_s", [128, 4], F32)
    cwc = sb("cwc_s", [128, 3 * CO], F32)
    cbc = sb("cbc_s", [128, CO], F32)
    m2col = sb("m2col", [128, 1], F32)
    r1col = sb("r1col", [128, 1], F32)
    r2col = sb("r2col", [128, 1], F32)
    m1col = sb("m1col", [128, 1], F32)
    acc1 = sb("acc1", [128, 1], F32)
    acc2 = sb("acc2", [128, 1], F32)
    tot11 = sb("tot11", [1, 1], F32)
    redt = sb("redt", [128, 1], F32)
    stt12_ = [sb(f"stt12_{j}", [128, 12], F32) for j in range(2)]
    sc4a_ = [sb(f"sc4a_{j}", [128, 4], F32) for j in range(2)]
    sc4b_ = [sb(f"sc4b_{j}", [128, 4], F32) for j in range(2)]
    sc4c_ = [sb(f"sc4c_{j}", [128, 4], F32) for j in range(2)]
    mu_c_ = [sb(f"mu_c_{j}", [128, 1], F32) for j in range(2)]
    pp_c_ = [sb(f"pp_c_{j}", [128, 1], F32) for j in range(2)]
    vv_c_ = [sb(f"vv_c_{j}", [128, 1], F32) for j in range(2)]
    rs_c_ = [sb(f"rs_c_{j}", [128, 1], F32) for j in range(2)]
    nrs_c_ = [sb(f"nrs_c_{j}", [128, 1], F32) for j in range(2)]
    nmr_c_ = [sb(f"nmr_c_{j}", [128, 1], F32) for j in range(2)]
    ln_idx = [0]
    amx_c = [sb(f"amx{j}", [128, 1], F32) for j in range(2)]
    s1_c = [sb(f"s1c{j}", [128, 1], F32) for j in range(2)]
    a1_c = [sb(f"a1c{j}", [128, 1], F32) for j in range(2)]
    am2_c = [sb(f"am2{j}", [128, 1], F32) for j in range(2)]
    s2_c = [sb(f"s2c{j}", [128, 1], F32) for j in range(2)]
    a2m2_c = [sb(f"a2m2{j}", [128, 1], F32) for j in range(2)]
    s2row = sb("s2row", [1, 128], F32)

    def ln_stats(src_ap, extra_v_waits):
        j = ln_idx[0] % 2
        ln_idx[0] += 1
        stt12, sc4a, sc4b, sc4c = stt12_[j], sc4a_[j], sc4b_[j], sc4c_[j]
        mu_c, pp_c, vv_c = mu_c_[j], pp_c_[j], vv_c_[j]
        rs_c, nrs_c, nmr_c = rs_c_[j], nrs_c_[j], nmr_c_[j]
        ls, lv, sf = f"_ls{j}", f"_lv{j}", f"_sf{j}"
        w = list(extra_v_waits) + [(sem(ls), None)]
        sem(lv), sem(sf)
        p.op("V", lambda v, a=src_ap, t=stt12: v.bn_stats(t[:, 0:6], a[:, 0:512]),
             w, inc=sem("_lf"))
        p.op("V", lambda v, a=src_ap, t=stt12: v.bn_stats(t[:, 6:12], a[:, 512:1024]),
             inc=sem("_lf"))
        means = stt12[:, 1:12:3]
        m2s = stt12[:, 2:12:3]
        p.op("V", lambda v, m=means, o=sc4a, mc=mu_c: v.tensor_scalar(
            o[:], m, 0.25, 0.0, ALU.mult, ALU.add, accum_out=mc[:]),
            [("_lf", None)], inc=sem("_lf"))
        p.op("V", lambda v, m=means, o=sc4b, pc=pp_c: v.scalar_tensor_tensor(
            o[:], m, 0.25, m, ALU.mult, ALU.mult, accum_out=pc[:]),
            inc=sem("_lf"))
        p.op("V", lambda v, m=m2s, o=sc4c, vc=vv_c: v.tensor_scalar(
            o[:], m, 1.0 / 1024.0, 0.0, ALU.mult, ALU.add, accum_out=vc[:]),
            inc=sem("_lf"))
        p.op("V", lambda v, pc=pp_c, vc=vv_c: v.tensor_add(pc[:], pc[:], vc[:]),
             [("_lf", None)], inc=sem("_lf"))
        p.op("V", lambda v, mc=mu_c, pc=pp_c, vc=vv_c: v.scalar_tensor_tensor(
            vc[:], mc[:], mc[:], pc[:], ALU.mult, ALU.subtract),
            [("_lf", None)])
        p.op("V", lambda v, nc_=nrs_c, mc=mu_c: v.tensor_sub(nc_[:], zec[:, 0:1],
                                                             mc[:]),
             inc=sem(lv))
        p.op("S", lambda s, vc=vv_c, rc=rs_c: s.activation(
            rc[:], vc[:], AF.Ln, bias=zec[:, 1:2], scale=zec[:, 3:4]),
            [(lv, None)],
            inc=sem(sf))
        p.op("S", lambda s, rc=rs_c: s.activation(
            rc[:], rc[:], AF.Exp, bias=0.0, scale=zec[:, 2:3]), [(sf, None)],
            inc=sem(sf))
        p.op("S", lambda s, nc_=nrs_c, mr=nmr_c, rc=rs_c: s.activation(
            mr[:], nc_[:], AF.Copy, bias=0.0, scale=rc[:]), [(sf, None)],
            inc=sem(sf))
        p.op("S", lambda s: s.nop(), [(sf, None)], inc=sem(ls))
        return rs_c, nmr_c

    actx = contextlib.ExitStack()
    EA = actx.enter_context

    def sba(name, shape, dt):
        return EA(nc.sbuf_tensor(name, shape, dt))

    def psa(name, shape):
        return EA(nc.psum_tensor(name, shape, F32))

    wqT = sba("wqT", [128, DC * 1024], BF16)
    wkT = sba("wkT", [128, DC * 1024], BF16)
    wvT = sba("wvT", [128, DC * 1024], BF16)
    bqc = sba("bqc_s", [128, DC], F32)
    bkc = sba("bkc_s", [128, DC], F32)
    bvb = sba("bvb", [128, D], F32)
    xbuf = [sba(f"xbuf{j}", [128, D], F32) for j in range(2)]
    x1n = [sba(f"x1n{j}", [128, D], BF16) for j in range(2)]
    x1nT = [sba(f"x1nT{j}", [128, D], BF16) for j in range(2)]
    qt_sb = sba("qt_sb", [128, D], BF16)
    kt_sb = sba("kt_sb", [128, D], BF16)
    v_sb = sba("v_sb", [128, D], BF16)
    p_sb = sba("p_sb", [128, 128], BF16)
    pt_sb = sba("pt_sb", [128, 128], BF16)
    o_sb = sba("o_sb", [128, D], F32)
    m_col = sba("m_col", [128, 1], F32)
    nm_col = sba("nm_col", [128, 1], F32)
    ss_col = sba("ss_col", [128, 1], F32)
    rr_col = sba("rr_col", [128, 1], F32)
    wstg = [sba(f"wstg{j}", [128, 1024], BF16) for j in range(2)]
    fstg = [sba(f"fstg{j}", [128, 1024], F32) for j in range(4)]
    tstg = [sba(f"tstg{j}", [128, 1024], BF16) for j in range(3)]
    ostg = [sba(f"ostg{j}", [128, 1024], BF16) for j in range(2)]

    qt_ps = psa("qt_ps", [128, 1024])
    kt_ps = psa("kt_ps", [128, 1024])
    v_ps = psa("v_ps", [128, 1024])
    fb = [psa(f"fb{j}", [128, 512]) for j in range(2)]
    fac = {"i": 0, "free": [[], []]}

    def fac_begin():
        b = fac["i"] % 2
        fac["i"] += 1
        return b, list(fac["free"][b])

    def fac_end(b, conds):
        fac["free"][b] = list(conds)

    for dst, src in ((idb, idb_h), (idf, idf_h), (onr, onr_h), (onc, onc_h),
                     (zec, zec_h), (cwc, cwc_h), (cbc, cbc_h), (bqc, bqc_h),
                     (bkc, bkc_h)):
        p.op("Y", lambda y, d=dst, s=src: y.dma_start(out=d[:], in_=s),
             inc=sem("cld"), amt=16)
    p.op("Y", lambda y: y.dma_start(out=bvb[0:1, :], in_=bvr_h),
         inc=sem("cld"), amt=16)
    CLD = p.n("cld")

    for h in range(2):
        p.op("T", lambda t, h=h: t.matmul(
            qt_ps[:, h * 512:(h + 1) * 512], onr[:],
            bvb[0:1, h * 512:(h + 1) * 512], start=True, stop=True),
            waits=[("cld", CLD)], inc=sem("bvt"))
    p.op("V", lambda v: v.tensor_copy(bvb[:], qt_ps[:]), waits=[("bvt", 2)],
         inc=sem("bvv"))

    for widx, (w_hbm, wT_sb) in enumerate(((wq_h, wqT), (wk_h, wkT), (wv_h, wvT)) if PREP else ()):
        for ec in range(DC):
            n = widx * 8 + ec
            slot = n % 2
            p.op("G", lambda g, s=slot, src=w_hbm, e=ec:
                 g.dma_start(out=wstg[s][:], in_=src[e * 128:(e + 1) * 128, :]),
                 waits=[("pT", max(0, (n - 1) * 8))],
                 inc=sem(f"gw{n % 2}"), amt=16)
            for d0 in range(0, DC, 4):
                b, fw = fac_begin()
                for j in range(4):
                    p.op("T", lambda t, s=slot, d=d0 + j, b=b, j=j:
                         t.matmul(fb[b][:, j * 128:(j + 1) * 128],
                                  wstg[s][:, d * 128:(d + 1) * 128],
                                  idb[:], start=True, stop=True),
                         waits=(fw if j == 0 else []) +
                               [(f"gw{n % 2}", 16 * (n // 2 + 1)), ("cld", CLD)],
                         inc=sem("pT"))
                p.op("S", lambda s_, dst=wT_sb, e=ec, d0=d0, b=b:
                     s_.copy(dst[:, e * 1024 + d0 * 128: e * 1024 + d0 * 128 + 512],
                             fb[b][:, 0:512]),
                     waits=[("pT", p.n("pT"))], inc=sem("pTe"))
                fac_end(b, [("pTe", p.n("pTe"))])
    QKV_PREP = p.n("pTe")

    fstg_free = [("cld", 0)] * 4
    nw_ld = [0]

    def mk_abs(src_ap, acc, first):
        pc = {}

        def load():
            n = nw_ld[0]
            nw_ld[0] += 1
            pc["n"] = n
            slot = n % 4
            p.op("Y", lambda y, s=slot, src=src_ap: y.dma_start(out=fstg[s][:],
                                                                in_=src),
                 waits=[fstg_free[slot]], inc=sem(f"yw{slot}"), amt=16)

        def mid():
            n = pc["n"]
            slot = n % 4
            p.op("V", lambda v, s=slot: v.reduce_sum(redt[:], fstg[s][:], AX.X,
                                                     apply_absolute_value=True),
                 waits=[(f"yw{slot}", 16 * (n // 4 + 1)),
                        ("absv", p.n("absv"))],
                 inc=sem("absr"))
            if first:
                p.op("G", lambda g, a=acc: g.tensor_copy(a[:], redt[:]),
                     waits=[("absr", p.n("absr"))], inc=sem("absv"))
            else:
                p.op("G", lambda g, a=acc: g.tensor_add(a[:], a[:], redt[:]),
                     waits=[("absr", p.n("absr"))], inc=sem("absv"))
            fstg_free[slot] = ("absv", p.n("absv"))

        def fin():
            pass

        return dict(load=load, mid=mid, fin=fin)

    def combine(acc, mcol, rcol, nabs):
        b, fw = fac_begin()
        p.op("T", lambda t, a=acc, b=b: t.matmul(
            fb[b][:1, 0:1], a[:], onc[:], start=True, stop=True),
            waits=fw + [("absv", nabs), ("cld", CLD)], inc=sem("pT"))
        p.op("V", lambda v, b=b: v.tensor_copy(tot11[:], fb[b][0:1, 0:1]),
             waits=[("pT", p.n("pT"))], inc=sem("_lf"))
        p.op("V", lambda v: v.tensor_scalar(tot11[:], tot11[:], 1.0 / (HID * D),
                                            1e-5, ALU.mult, ALU.max),
             waits=[("_lf", None)], inc=sem("cmbv"))
        fac_end(b, [("cmbv", p.n("cmbv"))])
        b2, fw2 = fac_begin()
        p.op("T", lambda t, b=b2: t.matmul(
            fb[b][:, 0:1], onr[:], tot11[:], start=True, stop=True),
            waits=fw2 + [("cmbv", p.n("cmbv"))], inc=sem("pT"))
        p.op("V", lambda v, d=mcol, b=b2: v.tensor_copy(d[:], fb[b][:, 0:1]),
             waits=[("pT", p.n("pT"))], inc=sem("_lf"))
        p.op("V", lambda v, d=rcol, mc=mcol: v.reciprocal(d[:], mc[:]),
             waits=[("_lf", None)], inc=sem("cmbv"))
        fac_end(b2, [("cmbv", p.n("cmbv"))])


    ntern = [0]

    def mk_tern(src_ap, rcol, scr, scol_off, mcl_need):
        pc = {}

        def load():
            n = nw_ld[0]
            nw_ld[0] += 1
            pc["n"] = n
            slot = n % 4
            p.op("Y", lambda y, s=slot, src=src_ap: y.dma_start(out=fstg[s][:],
                                                                in_=src),
                 waits=[fstg_free[slot]], inc=sem(f"yw{slot}"), amt=16)

        def mid():
            n = pc["n"]
            slot = n % 4
            nt = ntern[0]
            ntern[0] += 1
            pc["nt"] = nt
            ts_ = nt % 3
            p.op("V", lambda v, s=slot, r=rcol: v.tensor_scalar(
                fstg[s][:], fstg[s][:], r[:], MAGIC, ALU.mult, ALU.add),
                waits=[(f"yw{slot}", 16 * (n // 4 + 1)), ("cmbv", 2 * mcl_need)])
            p.op("V", lambda v, s=slot: v.tensor_scalar(
                fstg[s][:], fstg[s][:], MAGIC + 1.0, MAGIC - 1.0, ALU.min,
                ALU.max))
            p.op("V", lambda v, s=slot, t_=ts_: v.tensor_scalar(
                tstg[t_][:], fstg[s][:], MAGIC, None, ALU.subtract),
                waits=[("ternT", max(0, (nt - 2) * 8))], inc=sem("ternv"))
            fstg_free[slot] = ("ternv", p.n("ternv"))
            pc["tv"] = p.n("ternv")

        def fin():
            nt = pc["nt"]
            ts_ = nt % 3
            for d0 in range(0, DC, 4):
                b, fw = fac_begin()
                for j in range(4):
                    p.op("T", lambda t, t_=ts_, d=d0 + j, b=b, j=j:
                         t.matmul(fb[b][:, j * 128:(j + 1) * 128],
                                  tstg[t_][:, d * 128:(d + 1) * 128],
                                  idb[:], start=True, stop=True),
                         waits=(fw if j == 0 else []) + [("ternv", pc["tv"])],
                         inc=sem("ternT"))
                p.op("S", lambda s_, d0=d0, b=b, o=ostg[nt % 2]:
                     s_.copy(o[:, d0 * 128:d0 * 128 + 512], fb[b][:, 0:512]),
                     waits=[("ternT", p.n("ternT")),
                            (f"wst{nt % 2}", max(0, 16 * (nt // 2)))],
                     inc=sem("pTe"))
                fac_end(b, [("pTe", p.n("pTe"))])
            p.op("Y", lambda y, sc=scr, off=scol_off, o=ostg[nt % 2]:
                 y.dma_start(out=sc[:, off:off + 1024], in_=o[:]),
                 waits=[("pTe", p.n("pTe"))], inc=sem(f"wst{nt % 2}"), amt=16)

        return dict(load=load, mid=mid, fin=fin)

    w1_abs = [mk_abs(w1_h[co * 128:(co + 1) * 128, :], acc1, co == 0)
              for co in range(CO)]
    w2_abs = [mk_abs(w2_h[d * 128:(d + 1) * 128, c * 1024:(c + 1) * 1024], acc2,
                     d == 0 and c == 0) for d in range(DC) for c in range(4)]
    w1_tern = [mk_tern(w1_h[co * 128:(co + 1) * 128, :], r1col, w1sc,
                       co * 1024, 1) for co in range(CO)]
    w2_tern = [mk_tern(w2_h[d * 128:(d + 1) * 128, c * 1024:(c + 1) * 1024],
                       r2col, w2sc, d * 4096 + c * 1024, 2)
               for d in range(DC) for c in range(4)]

    def _cmb1fin():
        combine(acc1, m1col, r1col, 32)
        p.op("V", lambda v: v.tensor_scalar(cwc[:], cwc[:], m1col[:], None,
                                            ALU.mult),
             waits=[("cmbv", 2), ("cld", CLD)], inc=sem("cwr"))

    def _nostage():
        pass

    combine1 = dict(load=_nostage, mid=_nostage, fin=_cmb1fin)
    combine2 = dict(load=_nostage, mid=_nostage,
                    fin=lambda: combine(acc2, m2col, r2col, 64))

    prep_sched = {}
    for c in range(8 if PREP else 0):
        prep_sched.setdefault(c, []).extend(w1_abs[c * 4:(c + 1) * 4])
    if PREP:
        prep_sched.setdefault(8, []).append(combine1)
        for c in range(8, 16):
            prep_sched.setdefault(c, []).extend(w2_abs[(c - 8) * 4:(c - 7) * 4])
        prep_sched.setdefault(16, []).append(combine2)
        for i, un in enumerate(w1_tern):
            prep_sched.setdefault(9 + i // 3, []).append(un)
        for i, un in enumerate(w2_tern):
            prep_sched.setdefault(17 + i // 3, []).append(un)
    prep_pieces = []
    for c in sorted(prep_sched):
        prep_pieces.extend(prep_sched[c])
    pst = {"load": 0, "mid": 0, "fin": 0}

    def prep_advance():
        i_f = pst["fin"]
        np_ = len(prep_pieces)
        while pst["load"] <= min(i_f + 4, np_ - 1):
            prep_pieces[pst["load"]]["load"]()
            pst["load"] += 1
        while pst["mid"] <= min(i_f + 3, np_ - 1):
            prep_pieces[pst["mid"]]["mid"]()
            pst["mid"] += 1
        prep_pieces[i_f]["fin"]()
        pst["fin"] += 1

    def a_load(c):
        p.op("Y", lambda y, c=c: y.dma_start(
            out=xbuf[c % 2][:], in_=xin[c * 128:(c + 1) * 128, :]),
            waits=[(f"x2st{c % 2}", max(0, 16 * (c // 2)))],
            inc=sem(f"xld{c % 2}"), amt=16)

    def a_ln1(c):
        rs_c, nmr_c = ln_stats(xbuf[c % 2][:], [(f"xld{c % 2}", 16 * (c // 2 + 1))])
        p.op("S", lambda s, c=c, r=rs_c, m=nmr_c: s.activation(
            x1n[c % 2][:], xbuf[c % 2][:], AF.Identity, bias=m[:], scale=r[:]),
             waits=[("at1t", max(0, 8 * (c - 1)))], inc=sem("aln1"))

    def a_trans(c):
        for d0 in range(0, DC, 4):
            b, fw = fac_begin()
            for j in range(4):
                p.op("T", lambda t, c=c, dc=d0 + j, b=b, j=j:
                     t.matmul(fb[b][:, j * 128:(j + 1) * 128],
                              x1n[c % 2][:, dc * 128:(dc + 1) * 128], idb[:],
                              start=True, stop=True),
                     waits=(fw if j == 0 else []) + [("aln1", c + 1)],
                     inc=sem("at1t"))
            p.op("S", lambda s, c=c, d0=d0, b=b:
                 s.copy(x1nT[c % 2][:, d0 * 128:d0 * 128 + 512],
                        fb[b][:, 0:512]),
                 waits=[("at1t", p.n("at1t")), ("av_mm", max(0, c - 1))],
                 inc=sem("at1e"))
            fac_end(b, [("at1e", p.n("at1e"))])

    def a_qkv(c):
        xT = x1nT[c % 2]
        for ec in range(DC):
            for dc in range(DC):
                p.op("T", lambda t, e=ec, d=dc, x=xT:
                     t.matmul(qt_ps[:, e * 128:(e + 1) * 128],
                              wqT[:, e * 1024 + d * 128:e * 1024 + d * 128 + 128],
                              x[:, d * 128:(d + 1) * 128],
                              start=(d == 0), stop=(d == DC - 1)),
                     waits=[("at1e", 2 * (c + 1)), ("pTe", QKV_PREP),
                            ("aqe", max(0, c)), ("bvv", 1)],
                     inc=sem("aq") if (ec == DC - 1 and dc == DC - 1) else None)
        for ec in range(DC):
            for dc in range(DC):
                p.op("T", lambda t, e=ec, d=dc, x=xT:
                     t.matmul(kt_ps[:, e * 128:(e + 1) * 128],
                              wkT[:, e * 1024 + d * 128:e * 1024 + d * 128 + 128],
                              x[:, d * 128:(d + 1) * 128],
                              start=(d == 0), stop=(d == DC - 1)),
                     waits=[("ake", max(0, c))],
                     inc=sem("ak") if (ec == DC - 1 and dc == DC - 1) else None)
        wvr = wvT[:].rearrange("p (e x) -> p e x", x=1024)
        for dc in range(DC):
            for h in range(2):
                p.op("T", lambda t, d=dc, h=h, x=xT:
                     t.matmul(v_ps[:, h * 512:(h + 1) * 512],
                              x[:, d * 128:(d + 1) * 128],
                              wvr[:, h * 4:h * 4 + 4, d * 128:(d + 1) * 128],
                              start=(d == 0), stop=(d == DC - 1)),
                     waits=[("aoev", max(0, c))],
                     inc=sem("av_mm") if (dc == DC - 1 and h == 1) else None)

    def a_evac(c):
        for ec in range(DC):
            p.op("V", lambda v, e=ec: v.tensor_scalar(
                qt_sb[:, e * 128:(e + 1) * 128], qt_ps[:, e * 128:(e + 1) * 128],
                bqc[:, e:e + 1], None, ALU.add),
                waits=[("aq", c + 1), ("asc", max(0, c))],
                inc=sem("aqe") if ec == DC - 1 else None)
        for ec in range(DC):
            p.op("V", lambda v, e=ec: v.tensor_scalar(
                kt_sb[:, e * 128:(e + 1) * 128], kt_ps[:, e * 128:(e + 1) * 128],
                bkc[:, e:e + 1], None, ALU.add),
                waits=[("ak", c + 1)],
                inc=sem("ake") if ec == DC - 1 else None)
        p.op("V", lambda v: v.tensor_add(v_sb[:], v_ps[:], bvb[:]),
             waits=[("av_mm", c + 1), ("bvv", 1), ("aav", max(0, c))],
             inc=sem("ave"))

    def a_attn(c):
        bS, fwS = fac_begin()
        for ec in range(DC):
            p.op("T", lambda t, e=ec, b=bS: t.matmul(
                fb[b][:, 0:128], qt_sb[:, e * 128:(e + 1) * 128],
                kt_sb[:, e * 128:(e + 1) * 128],
                start=(e == 0), stop=(e == DC - 1)),
                waits=(fwS if ec == 0 else []) + [("aqe", c + 1), ("ake", c + 1)],
                inc=sem("asc") if ec == DC - 1 else None)
        p.op("V", lambda v, b=bS: v.reduce_max(nm_col[:], fb[b][:, 0:128], AX.X,
                                               negate=True),
             waits=[("asc", c + 1)], inc=sem("avmax"))
        p.op("S", lambda s, b=bS: s.activation(p_sb[:], fb[b][:, 0:128], AF.Exp,
                                               bias=nm_col[:], scale=1.0),
             waits=[("avmax", c + 1), ("apt", max(0, c))], inc=sem("aexp"))
        fac_end(bS, [("avmax", p.n("avmax")), ("aexp", p.n("aexp"))])
        bP, fwP = fac_begin()
        p.op("T", lambda t, b=bP: t.matmul(fb[b][:, 0:128], p_sb[:], idb[:],
                                           start=True, stop=True),
             waits=fwP + [("aexp", c + 1)], inc=sem("apt"))
        p.op("V", lambda v, b=bP: v.tensor_copy(pt_sb[:], fb[b][:, 0:128]),
             waits=[("apt", c + 1), ("aav", max(0, c))], inc=sem("apte"))
        fac_end(bP, [("apte", p.n("apte"))])
        for h in range(2):
            p.op("T", lambda t, h=h: t.matmul(
                v_ps[:, h * 512:(h + 1) * 512], pt_sb[:],
                v_sb[:, h * 512:(h + 1) * 512], start=True, stop=True),
                waits=[("apte", c + 1), ("ave", c + 1)],
                inc=sem("aav") if h == 1 else None)
        p.op("S", lambda s: s.copy(o_sb[:], v_ps[:]),
             waits=[("aav", c + 1), ("ax2v", max(0, c))],
             inc=sem("aoev"))

    def a_ln2(c):
        rs_c, nmr_c = ln_stats(o_sb[:], [("aoev", c + 1)])
        p.op("V", lambda v, c=c, m=nmr_c: v.tensor_scalar(
            xbuf[c % 2][:], xbuf[c % 2][:], m[:], None, ALU.add),
            waits=[(f"_ls{(ln_idx[0] - 1) % 2}", None)])
        p.op("V", lambda v, c=c, r=rs_c: v.scalar_tensor_tensor(
            xbuf[c % 2][:], o_sb[:], r[:], xbuf[c % 2][:], ALU.mult, ALU.add),
            inc=sem("ax2v"))
        p.op("Y", lambda y, c=c: y.dma_start(
            out=x2sc[c * 128:(c + 1) * 128, :], in_=xbuf[c % 2][:]),
            waits=[("ax2v", c + 1)], inc=sem(f"x2st{c % 2}"), amt=16)

    a_load(0)
    a_load(1)
    a_ln1(0)
    a_trans(0)
    for c in range(NCH):
        if c + 1 < NCH:
            a_ln1(c + 1)
        a_qkv(c)
        a_evac(c)
        a_attn(c)
        if c + 1 < NCH:
            a_trans(c + 1)
        a_ln2(c)
        if c + 2 < NCH:
            a_load(c + 2)
        for _ in prep_sched.get(c, []):
            prep_advance()

    abh = sem("ab")
    for e in ("T", "V", "S", "G", "Y"):
        p.op(e, lambda eng: eng.sem_inc(sems["ab"], 1),
             waits=[("x2st0", None), ("x2st1", None),
                    ("wst0", None), ("wst1", None)] if e == "Y" else [],
             inc="ab", raw_inc=True)

    actx.close()

    bctx = contextlib.ExitStack()
    EB = bctx.enter_context

    def sbb(name, shape, dt):
        return EB(nc.sbuf_tensor(name, shape, dt))

    def psb(name, shape):
        return EB(nc.psum_tensor(name, shape, F32))

    w1qT = sbb("w1qT", [128, CO * 1024], BF16)
    w2qT = sbb("w2qT", [128, DC * 4096], BF16)
    x2t = sbb("x2t", [128, 2 * D], F32)
    g_sb = sbb("g_sb", [128, CO * TB], F32)
    xqfm = [sbb(f"xqfm{j}", [128, DC * XQS], BF16) for j in range(3)]
    xqtm = [sbb(f"xqtm{j}", [128, D], BF16) for j in range(2)]
    fu = sbb("fu", [128, D], F32)
    scr1k = [sbb(f"scr1k{j}", [128, D], F32) for j in range(2)]
    c1buf = sbb("c1buf", [128, TB], F32)
    ybuf = [sbb(f"ybuf{j}", [128, TB], F32) for j in range(2)]
    xq2 = [sbb(f"xq2{j}", [128, 512], BF16) for j in range(2)]
    mred = scr1k[1][:, 256:512]
    s2b = sbb("s2b", [128, 512], F32)

    h_ps = [psb(f"h_ps{j}", [128, 512]) for j in range(2)]
    o_ps = [psb(f"o_ps{j}", [128, 1024]) for j in range(2)]
    f_ps = [psb(f"f_ps{j}", [128, 512]) for j in range(2)]
    hps4 = [h_ps[0], h_ps[1], f_ps[0], f_ps[1]]

    AB = [("ab", 5)]

    p.op("Y", lambda y: y.dma_start(out=w1qT[:], in_=w1sc),
         waits=AB, inc=sem("wb1"), amt=16)
    p.op("Y", lambda y: y.dma_start(out=w2qT[:], in_=w2sc), inc=sem("wb2"), amt=16)
    p.op("Y", lambda y: y.dma_start(out=out_h, in_=x2sc), inc=sem("bcp"), amt=16)

    def b_load(i):
        for tc in range(2):
            r0 = i * TB + tc * 128
            p.op("Y", lambda y, i=i, tc=tc, r0=r0:
                 y.dma_start(out=x2t[:, tc * D:(tc + 1) * D],
                             in_=x2sc[r0:r0 + 128, :]),
                 waits=AB + [("bfu", max(0, 2 * i))],
                 inc=sem(f"bx{i % 2}"), amt=16)

    def b_front(f):
        buf = xqfm[f % 3]
        xq_n = [0, 0]
        for b in range(2):
            blk = 2 * f + b
            src = x2t[:, b * D:(b + 1) * D]
            rs_c, nmr_c = ln_stats(src, AB + [(f"bx{f % 2}", 32 * (f // 2 + 1))])
            p.op("S", lambda s, src=src, b=b, r=rs_c, m=nmr_c: s.activation(
                fu[:], src, AF.Identity, bias=m[:], scale=r[:]),
                waits=[("bfxq", max(0, blk - 1))],
                inc=sem("bfu"))
            p.op("V", lambda v, b=b: v.reduce_max(amx_c[b][:], fu[:], AX.X,
                                                  apply_absolute_value=True),
                 waits=[("bfu", p.n("bfu")), ("bfb", max(0, blk - 1))],
                 inc=sem("_lf"))
            p.op("V", lambda v, b=b: v.tensor_scalar(amx_c[b][:], amx_c[b][:],
                                                     EPS, None, ALU.max),
                 waits=[("_lf", None)], inc=sem("_lf"))
            p.op("V", lambda v, b=b: v.reciprocal(s1_c[b][:], amx_c[b][:]),
                 waits=[("_lf", None)], inc=sem("_lf"))
            p.op("V", lambda v, b=b: v.tensor_scalar(s1_c[b][:], s1_c[b][:],
                                                     127.0, None, ALU.mult),
                 waits=[("_lf", None)], inc=sem("_lf"))
            p.op("V", lambda v, b=b: v.tensor_scalar(a1_c[b][:], amx_c[b][:],
                                                     1.0 / 127.0, None, ALU.mult),
                 waits=[("_lf", None)], inc=sem("bfam"))
            p.op("S", lambda s, b=b: s.activation(
                fu[:], fu[:], AF.Copy, bias=MAGIC, scale=s1_c[b][:]),
                waits=[("bfam", p.n("bfam"))], inc=sem("bfb"))
            p.op("V", lambda v, b=b: v.tensor_scalar(
                xqtm[b][:], fu[:], MAGIC, a1_c[b][:], ALU.subtract, ALU.mult),
                waits=[("bfb", p.n("bfb")), ("bftT", max(0, blk * 8 - 8))],
                inc=sem("bfxq"))
            xq_n[b] = p.n("bfxq")

        def emit_T():
            bv = buf[:].rearrange("p (d c) -> p d c", c=XQS)
            for b in range(2):
                blk = 2 * f + b
                for d0 in range(0, DC, 4):
                    n2 = blk * 2 + d0 // 4
                    bank = n2 % 2
                    for j in range(4):
                        p.op("T", lambda t, b=b, dc=d0 + j, bank=bank, j=j:
                             t.matmul(f_ps[bank][:, j * 128:(j + 1) * 128],
                                      xqtm[b][:, dc * 128:(dc + 1) * 128],
                                      idb[:], start=True, stop=True),
                             waits=AB + [("bfxq", xq_n[b]),
                                         ("bfte", max(0, n2 - 1)),
                                         ("bconvy", max(0, 32 * f - 32))],
                             inc=sem("bftT"))
                    p.op("S", lambda s, d0=d0, b=b, bank=bank, bv=bv:
                         s.copy(bv[:, d0:d0 + 4, 1 + b * 128: 1 + b * 128 + 128],
                                f_ps[bank][:, 0:512]),
                         waits=[("bftT", p.n("bftT")), ("bhl", max(0, f - 1)),
                                ("bhmm", max(0, 32 * (f - 2)))],
                         inc=sem("bfte"))
        return emit_T

    def b_halo_left(i):
        cur = xqfm[i % 3][:].rearrange("p (d c) -> p d c", c=XQS)
        if i % TPS == 0:
            p.op("V", lambda v, cur=cur: v.tensor_scalar(
                cur[:, :, 0:1][:, :, 0], idf[:, 0:DC], 0.0, None, ALU.mult),
                waits=[("bfte", 4 * (i + 1)), ("cld", CLD),
                       ("bhmm", max(0, 32 * (i - 2)))], inc=sem("bhl"))
        else:
            prev = xqfm[(i - 1) % 3][:].rearrange("p (d c) -> p d c", c=XQS)
            p.op("V", lambda v, cur=cur, prev=prev: v.tensor_copy(
                cur[:, :, 0:1][:, :, 0], prev[:, :, 256:257][:, :, 0]),
                waits=[("bfte", 4 * (i + 1)),
                       ("bhmm", max(0, 32 * (i - 2)))], inc=sem("bhl"))

    def b_halo_right(i):
        cur = xqfm[i % 3][:].rearrange("p (d c) -> p d c", c=XQS)
        if i % TPS == TPS - 1:
            p.op("V", lambda v, cur=cur: v.tensor_scalar(
                cur[:, :, 257:258][:, :, 0], idf[:, 0:DC], 0.0, None, ALU.mult),
                waits=[("bhl", i + 1), ("cld", CLD)], inc=sem("bhr"))
        else:
            nxt = xqfm[(i + 1) % 3][:].rearrange("p (d c) -> p d c", c=XQS)
            p.op("V", lambda v, cur=cur, nxt=nxt: v.tensor_copy(
                cur[:, :, 257:258][:, :, 0], nxt[:, :, 1:2][:, :, 0]),
                waits=[("bfte", 4 * (i + 2)), ("bhl", i + 1)], inc=sem("bhr"))

    def b_coloop(i, pend=None):
        buf = xqfm[i % 3]
        for co2 in range(CO // 2):
            pair = (2 * co2, 2 * co2 + 1)
            for co in pair:
                k = 32 * i + co
                hp = hps4[co % 4]
                for dc in range(DC):
                    w = [("bhl", i + 1), ("bhr", i + 1), ("wb1", 16),
                         ("bconvy", max(0, k - 3))]
                    if co == 0:
                        w.append(("bmsqv", max(0, 2 * i)))
                    if co == 1:
                        w += [("bs2r", max(0, 2 * i)), ("bs2e", max(0, 4 * i))]
                    if co in (2, 3):
                        w.append(("bfte", min(4 * (i + 2), 4 * NT)))
                    p.op("T", lambda t, co=co, dc=dc, hp=hp, buf=buf:
                         t.matmul(hp[:, 0:258],
                                  w1qT[:, co * 1024 + dc * 128:
                                       co * 1024 + dc * 128 + 128],
                                  buf[:, dc * XQS: dc * XQS + 258],
                                  start=(dc == 0), stop=(dc == DC - 1)),
                         waits=w,
                         inc=sem("bhmm") if dc == DC - 1 else None)
            for co in pair:
                k = 32 * i + co
                hp = hps4[co % 4]
                p.op("V", lambda v, co=co, hp=hp: v.tensor_scalar(
                    ybuf[co % 2][:], hp[:, 0:256], cwc[:, 3 * co:3 * co + 1],
                    None, ALU.mult),
                    waits=[("bhmm", k + 1), ("cwr", 1),
                           ("bsilu", max(0, k - 1))])
                p.op("V", lambda v, co=co, hp=hp: v.scalar_tensor_tensor(
                    c1buf[:], hp[:, 1:257], cwc[:, 3 * co + 1:3 * co + 2],
                    ybuf[co % 2][:], ALU.mult, ALU.add))
                p.op("V", lambda v, co=co, hp=hp: v.scalar_tensor_tensor(
                    ybuf[co % 2][:], hp[:, 2:258], cwc[:, 3 * co + 2:3 * co + 3],
                    c1buf[:], ALU.mult, ALU.add),
                    inc=sem("bconvy"))
            for co in pair:
                k = 32 * i + co
                p.op("S", lambda s, co=co, i=i: s.activation(
                    g_sb[:, co * TB:(co + 1) * TB], ybuf[co % 2][:], AF.Silu,
                    bias=cbc[:, co:co + 1], scale=1.0),
                    waits=[("bconvy", k + 1),
                           ("bxq2", max(0, 16 * (i - 1) + co // 2 + 1))],
                    inc=sem("bsilu"))
            if co % 8 == 7:
                q = co // 8
                gq = g_sb[:].rearrange("p (co t) -> p t co", t=TB)[:, :, q * 8:(q + 1) * 8]
                dst = mred if q == 0 else scr1k[1][:, 0:TB]
                wq = [("bsilu", 32 * i + 8 * (q + 1))]
                if q == 0:
                    wq.append(("bacc", max(0, 16 * (2 * (i - 1) + 2))))
                if q == 0:
                    wq.append(("bmsqT", max(0, 2 * i)))
                p.op("V", lambda v, gq=gq, dst=dst: v.reduce_max(
                    dst[:], gq, AX.X, apply_absolute_value=True), wq,
                    inc=sem("_lf"))
                if q > 0:
                    p.op("V", lambda v: v.tensor_max(mred[:], mred[:],
                                                     scr1k[1][:, 0:TB]),
                         [("_lf", None)],
                         inc=sem("bmrd") if q == 3 else None)
        if pend is not None:
            pend()

    def b_quant2(i):
        for tc in range(2):
            dstA = h_ps[0][:, 384:512] if tc == 0 else h_ps[0][:, 256:384]
            p.op("T", lambda t, tc=tc, dstA=dstA: t.matmul(
                dstA, mred[:, tc * 128:(tc + 1) * 128], idf[:],
                start=True, stop=True),
                waits=[("bmrd", i + 1), ("bmsqv", max(0, 2 * i + tc)),
                       ("bhmm", 32 * (i + 1))],
                inc=sem("bmsqT"))
            p.op("V", lambda v, tc=tc, dstA=dstA: v.reduce_max(
                am2_c[tc][:], dstA, AX.X),
                waits=[("bmsqT", 2 * i + tc + 1),
                       ("bfev", max(0, 2 * (i - 1) + tc + 1)),
                       ("bs2T", max(0, 2 * (i - 1) + tc + 1))],
                inc=sem("_lf"))
            p.op("V", lambda v, tc=tc: v.tensor_scalar(am2_c[tc][:], am2_c[tc][:],
                                                       EPS, None, ALU.max),
                 waits=[("_lf", None)], inc=sem("_lf"))
            p.op("V", lambda v, tc=tc: v.reciprocal(s2_c[tc][:], am2_c[tc][:]),
                 waits=[("_lf", None)], inc=sem("_lf"))
            p.op("V", lambda v, tc=tc: v.tensor_scalar(s2_c[tc][:], s2_c[tc][:],
                                                       127.0, None, ALU.mult),
                 waits=[("_lf", None)], inc=sem("_lf"))
            p.op("V", lambda v, tc=tc: v.tensor_scalar(a2m2_c[tc][:], am2_c[tc][:],
                                                       1.0 / 127.0, None, ALU.mult),
                 waits=[("_lf", None)], inc=sem("_lf"))
            p.op("V", lambda v, tc=tc: v.tensor_mul(a2m2_c[tc][:], a2m2_c[tc][:],
                                                    m2col[:]),
                 waits=[("_lf", None)], inc=sem("bmsqv"))
            p.op("T", lambda t, tc=tc: t.matmul(
                h_ps[1][:1, 256:384], s2_c[tc][:], idf[:], start=True, stop=True),
                waits=[("bmsqv", p.n("bmsqv")), ("bs2r", max(0, 2 * i + tc - 1)),
                       ("bs2e", None), ("bhmm", 32 * (i + 1))],
                inc=sem("bs2T"))
            p.op("V", lambda v: v.tensor_copy(s2row[:], h_ps[1][0:1, 256:384]),
                 waits=[("bs2T", p.n("bs2T")), ("bs2bT", None)], inc=sem("bs2r"))
            p.op("T", lambda t: t.matmul(h_ps[1][:, 384:512], onr[:], s2row[:],
                                         start=True, stop=True),
                 waits=[("bs2r", p.n("bs2r")),
                        ("bs2e", max(0, 4 * i + 2 * tc))],
                 inc=sem("bs2bT"))
            for rep in range(2):
                p.op("V", lambda v, tc=tc, rep=rep: v.tensor_copy(
                    s2b[:, rep * 256 + tc * 128: rep * 256 + tc * 128 + 128],
                    h_ps[1][:, 384:512]),
                    waits=[("bs2bT", p.n("bs2bT")), ("bxq2", None)],
                    inc=sem("bs2e"))


    def b_ffn2(i):
        w2r = w2qT[:].rearrange("p (d x) -> p d x", x=4096)
        for grp in range(16):
            gk = 16 * i + grp
            p.op("V", lambda v, grp=grp: v.tensor_mul(
                scr1k[0][:, 0:512], g_sb[:, grp * 512:(grp + 1) * 512], s2b[:]),
                waits=([("bacc", max(0, 16 * (2 * (i - 1) + 1)))]
                       if grp == 0 else []) +
                      [("bs2e", 4 * (i + 1)),
                       ("bwmm", max(0, (32 * i + 2 * grp) - 2))])
            p.op("V", lambda v, grp=grp: v.tensor_scalar(
                xq2[grp % 2][:], scr1k[0][:, 0:512], MAGIC, MAGIC, ALU.add,
                ALU.subtract),
                inc=sem("bxq2"))
            for ci in range(2):
                co = grp * 2 + ci
                for tc in range(2):
                    for h in range(2):
                        w = [("bxq2", gk + 1), ("wb2", 16)]
                        if co == 0:
                            w.append(("bfev", max(0, 2 * i)))
                        p.op("T", lambda t, co=co, ci=ci, tc=tc, h=h, w2r=w2r,
                             grp=grp:
                             t.matmul(o_ps[tc][:, h * 512:(h + 1) * 512],
                                      xq2[grp % 2][:, ci * 256 + tc * 128:
                                                    ci * 256 + tc * 128 + 128],
                                      w2r[:, h * 4:h * 4 + 4,
                                          co * 128:(co + 1) * 128],
                                      start=(co == 0), stop=(co == CO - 1)),
                             waits=w,
                             inc=sem("bwmm") if (tc == 1 and h == 1) else None)

    def b_ln4(i):
        for tc in range(2):
            p.op("S", lambda s, tc=tc: s.activation(
                scr1k[tc][:], o_ps[tc][:], AF.Copy, bias=0.0,
                scale=a2m2_c[tc][:]),
                waits=[("bwmm", 32 * (i + 1)), ("bmsqv", 2 * (i + 1)),
                       ("bacc", max(0, 16 * (2 * (i - 1) + tc + 1)))],
                inc=sem("bfev"))
            rs_c, nmr_c = ln_stats(scr1k[tc][:], [("bfev", p.n("bfev"))])
            p.op("S", lambda s, tc=tc, r=rs_c, m=nmr_c: s.activation(
                scr1k[tc][:], scr1k[tc][:], AF.Identity, bias=m[:],
                scale=r[:]), inc=sem("bfn"))
            p.op("G", lambda g, tc=tc, i=i: g.dma_start(
                out=out_h[i * TB + tc * 128: i * TB + tc * 128 + 128, :],
                in_=scr1k[tc][:], accum_op=ALU.add),
                waits=[("bfn", p.n("bfn")), ("bcp", 16)],
                inc=sem("bacc"), amt=16)

    b_load(0)
    b_front(0)()
    b_load(1)
    b_front(1)()
    for i in range(NT):
        b_halo_left(i)
        b_halo_right(i)
        pend = None
        if i + 2 < NT:
            b_load(i + 2)
            pend = b_front(i + 2)
        b_coloop(i, pend)
        b_quant2(i)
        b_ffn2(i)
        b_ln4(i)

    if DBGX2:
        for c in range(NCH):
            p.op("Y", lambda y, c=c: y.dma_start(
                out=x2t[:, 0:D], in_=x2sc[c * 128:(c + 1) * 128, :]),
                waits=AB, inc=sem("dbg"), amt=16)
            p.op("Y", lambda y, c=c: y.dma_start(
                out=out_h[c * 128:(c + 1) * 128, :], in_=x2t[:, 0:D]),
                waits=[("dbg", 16 * (2 * c + 1))], inc=sem("dbg"), amt=16)
        p.op("Y", lambda y: y.nop(), waits=[("dbg", 32 * NCH)])
    p.op("Y", lambda y: y.nop(), waits=[("bacc", 16 * 2 * NT)])

    bctx.close()

    with nc.Block() as block:
        def mk(ename):
            def run(eng):
                for seq, fn, waits, inc, amt in p.ops[ename]:
                    if TRUNC is not None and seq > TRUNC:
                        continue
                    for s, v in waits:
                        eng.wait_ge(sems[s], v)
                    ins = fn(eng)
                    if inc is not None:
                        ins.then_inc(sems[inc], amt)
            return run
        block.sync(mk("Y"))
        block.gpsimd(mk("G"))
        block.tensor(mk("T"))
        block.vector(mk("V"))
        block.scalar(mk("S"))

    globals()["_SEMS"] = {k: v.num for k, v in sems.items()}
    ctx.close()
    return nc


_NC_CACHE = []


def kernel(L0_attn_x, L0_attn_Wq, L0_attn_bq, L0_attn_Wk, L0_attn_bk,
           L0_attn_Wv, L0_attn_bv, L0_ffn_w1, L0_ffn_conv_w, L0_ffn_conv_b,
           L0_ffn_w2):
    x = np.ascontiguousarray(np.asarray(L0_attn_x, dtype=np.float32))
    B, S, _ = x.shape
    assert (B, S) == (16, 2048)

    def colsplit(vec, n):
        return np.ascontiguousarray(np.asarray(vec, np.float32).reshape(n, 128).T)

    common = {
        "wq": np.ascontiguousarray(np.asarray(L0_attn_Wq, np.float32)),
        "wk": np.ascontiguousarray(np.asarray(L0_attn_Wk, np.float32)),
        "wv": np.ascontiguousarray(np.asarray(L0_attn_Wv, np.float32)),
        "w1": np.ascontiguousarray(np.asarray(L0_ffn_w1, np.float32)),
        "w2": np.ascontiguousarray(np.asarray(L0_ffn_w2, np.float32)),
        "bqc": colsplit(L0_attn_bq, DC),
        "bkc": colsplit(L0_attn_bk, DC),
        "bvr": np.ascontiguousarray(
            np.asarray(L0_attn_bv, np.float32).reshape(1, D)),
        "cwc": np.ascontiguousarray(
            np.asarray(L0_ffn_conv_w, np.float32).reshape(HID, 3)
            .reshape(CO, 128, 3).transpose(1, 0, 2).reshape(128, 3 * CO)),
        "cbc": np.ascontiguousarray(
            np.asarray(L0_ffn_conv_b, np.float32).reshape(CO, 128).T),
        "idb": np.eye(128, dtype=ml_dtypes.bfloat16),
        "idf": np.eye(128, dtype=np.float32),
        "onr": np.ones((1, 128), dtype=np.float32),
        "onc": np.ones((128, 1), dtype=np.float32),
        "zec": np.concatenate([np.zeros((128, 1), np.float32),
                               np.full((128, 1), EPS, np.float32),
                               np.full((128, 1), -0.5, np.float32),
                               np.full((128, 1), -1.0, np.float32)], axis=1),
    }

    if not _NC_CACHE:
        _NC_CACHE.append(build_program())
    nc = _NC_CACHE[0]
    in_maps = []
    for c in range(8):
        m = dict(common)
        m["x"] = np.ascontiguousarray(x[2 * c:2 * c + 2].reshape(TOK, D))
        in_maps.append(m)
    res = run_bass_kernel_spmd(nc, in_maps, core_ids=list(range(8)))
    out = np.concatenate([np.asarray(r["out"]).reshape(2, S, D)
                          for r in res.results], axis=0)
    return out
